# version 02
# baseline (speedup 1.0000x reference)
import numpy as np
import ml_dtypes
import concourse.bass as bass
import concourse.mybir as mybir
from concourse.bass_utils import run_bass_kernel_spmd

F32 = mybir.dt.float32
BF16 = mybir.dt.bfloat16
AF = mybir.ActivationFunctionType
ALU = mybir.AluOpType
AX = mybir.AxisListType
bfloat16 = ml_dtypes.bfloat16

D = 1024
DC = 8
HID = 2048
NJC = 16
NE = 8
TOWN = 4096
NG = 4
GT = 1024
GW0 = GT + 128

_CACHE = {}
SERIAL_DMA = False


class Sem:
    def __init__(self, nc, name):
        self.h = nc.alloc_semaphore(name)
        self.n = 0


class Prog:
    def __init__(self, nc):
        self.nc = nc
        self.st = {"T": [], "V": [], "S": [], "Y": [], "G": []}
        self.T = Sem(nc, "sT")
        self.V = Sem(nc, "sV")
        self.S = Sem(nc, "sS")
        self.Y = Sem(nc, "sY")
        self.G = Sem(nc, "sG")
        self.SQ = Sem(nc, "sSQ")
        self.CC = Sem(nc, "sCC")
        for nm in ("ct", "cw", "cr", "wg", "xlA", "xlB", "winS0", "winS1",
                   "aw", "xrl", "bflA", "bflB", "xs", "sp0", "sp1", "sp2",
                   "sp3", "cc2", "oA", "oB", "wq0", "wq1", "stA", "stB"):
            setattr(self, nm, Sem(nc, "s_" + nm))

    def wait(self, s, sem, tok):
        if tok and tok > 0:
            self.st[s].append(lambda e, h=sem.h, t=tok: e.wait_ge(h, t))

    def waitt(self, s, t):
        if t:
            sem, v = t
            self.wait(s, sem, v)

    def op(self, s, fn, inc=False):
        if s in ("V", "S"):
            sem = getattr(self, s)
            self.wait(s, sem, sem.n)
            sem.n += 1
            self.st[s].append(lambda e, f=fn, h=sem.h: f(e).then_inc(h, 1))
            return sem.n
        if inc:
            sem = getattr(self, s)
            sem.n += 1
            self.st[s].append(lambda e, f=fn, h=sem.h: f(e).then_inc(h, 1))
            return sem.n
        self.st[s].append(fn)
        return None

    def dma(self, s, dst, src, rs=None):
        sem = rs if rs is not None else {"Y": self.Y, "G": self.G,
                                         "S": self.SQ}[s]
        sem.n += 16
        self.st[s].append(
            lambda e, d=dst, i=src, h=sem.h:
            e.dma_start(out=d, in_=i).then_inc(h, 16))
        if SERIAL_DMA:
            tok = sem.n
            self.st[s].append(lambda e, h=sem.h, t=tok: e.wait_ge(h, t))
        return sem.n


def build_program(stage=5, debug=False):
    nc = bass.Bass("TRN2", target_bir_lowering=False, debug=False)
    p = Prog(nc)

    def din(name, shape, dt=F32):
        return nc.dram_tensor(name, list(shape), dt, kind="ExternalInput").ap()

    d_xin = din("xin", [128 + TOWN, D])
    d_win = din("win_t", [D, 2 * HID])
    d_w1 = din("w1_t", [D, 2 * HID])
    d_wout = din("wout_t", [HID, D])
    d_w2 = din("w2_t", [NE, HID, D])
    d_wge = din("wge_t", [D, 10])
    d_gbeb = din("gbeb", [128, 10])
    d_wk = din("wk_t", [D, D])
    d_wv = din("wv_t", [D, D])
    d_wr = din("wr_t", [D, D])
    d_wo = din("wo_t", [D, D])
    d_tvec = din("tvec", [5, DC, 128, 1])
    d_mask = din("masks", [128, 2])
    d_ident = din("ident", [128, 128], BF16)
    d_y = nc.dram_tensor("y", [TOWN, D], F32, kind="ExternalOutput").ap()
    if debug:
        dbg_x1 = nc.dram_tensor("dbg_x1", [TOWN, D], F32,
                                kind="ExternalOutput").ap()
        dbg_x2 = nc.dram_tensor("dbg_x2", [TOWN, D], F32,
                                kind="ExternalOutput").ap()
        dbg_sm = nc.dram_tensor("dbg_sm", [128, 48], F32,
                                kind="ExternalOutput").ap()
        dbg_ekv = nc.dram_tensor("dbg_ekv", [128, TOWN], F32,
                                 kind="ExternalOutput").ap()
        dbg_xlnT = nc.dram_tensor("dbg_xlnT", [128, GW0], BF16,
                                  kind="ExternalOutput").ap()
        dbg_h = nc.dram_tensor("dbg_h", [128, GW0], BF16,
                               kind="ExternalOutput").ap()
        dbg_acc = nc.dram_tensor("dbg_acc", [128, D], F32,
                                 kind="ExternalOutput").ap()
        dbg_wse = nc.dram_tensor("dbg_wse", [128, 72], F32,
                                 kind="ExternalOutput").ap()
        dbg_w = nc.dram_tensor("dbg_w", [128, 4096], BF16,
                               kind="ExternalOutput").ap()
        dbg_rt = nc.dram_tensor("dbg_rt", [128, 9 * 64], F32,
                                kind="ExternalOutput").ap()

    def scr(name, shape, dt):
        return nc.dram_tensor(name, list(shape), dt, kind="Internal").ap()

    s_wb = {
        "win": scr("winb", [D, 2 * HID], BF16),
        "w1": scr("w1b", [D, 2 * HID], BF16),
        "wout": scr("woutb", [HID, D], BF16),
        "wk": scr("wkb", [D, D], BF16),
        "wv": scr("wvb", [D, D], BF16),
        "wr": scr("wrb", [D, D], BF16),
        "wo": scr("wob", [D, D], BF16),
    }
    for e_ in range(NE):
        s_wb[f"w2_{e_}"] = scr(f"w2b{e_}", [HID, D], BF16)
    s_x1 = scr("x1s", [TOWN, D], F32)
    s_x2 = scr("x2s", [TOWN, D], F32)
    s_xr = scr("xrs", [DC, 128, TOWN], BF16)
    s_ekv = scr("ekvs", [DC, 128, TOWN], BF16)
    s_ek = scr("eks", [DC, 128, TOWN], BF16)
    cc_in = nc.dram_tensor("cc_in", [128, 16], F32, kind="Internal")
    cc_out = nc.dram_tensor("cc_out", [128, 16], F32, kind="Internal")

    def sb(name, shape, dt):
        return nc.alloc_sbuf_tensor(name, list(shape), dt)

    b_x = sb("b_x", [128, 2 * D], F32)
    b_xln = sb("b_xln", [128, 2 * D], BF16)
    b_xlnT = sb("b_xlnT", [128, DC * GW0], BF16)
    b_h = sb("b_h", [128, NJC * GW0], BF16)
    b_acc = sb("b_acc", [128, 9 * D], F32)
    b_accb = b_acc.ap().bitcast(BF16)
    b_win = sb("b_win", [128, 2 * DC * 256], BF16)
    b_winf = b_win.ap().bitcast(F32)
    b_W = sb("b_W", [128, 2 * 16 * D], BF16)
    b_F = sb("b_F", [128, 5 * 1032], F32)
    b_bfl = sb("b_bfl", [128, 4 * D], BF16)
    b_conv = sb("b_conv", [128, 1024], BF16)
    b_wge = sb("b_wge", [128, DC * 10], BF16)
    b_gbeb = sb("b_gbeb", [128, 10], F32)
    b_tv = sb("b_tv", [128, 5 * DC], F32)
    b_drv = sb("b_drv", [128, 3 * DC], F32)
    b_mask = sb("b_mask", [128, 2], F32)
    b_id = sb("b_id", [128, 128], BF16)
    b_ends = sb("b_ends", [128, 16], F32)
    b_cin = sb("b_cin", [128, 16], F32)
    b_stage = sb("b_stage", [128, 16], F32)
    b_bnd = sb("b_bnd", [128, DC], BF16)
    b_wse = sb("b_wse", [128, 9 * 8], F32)
    b_rt = sb("b_rt", [128, 9 * 64], F32)
    b_st = sb("b_st", [128, 24], F32)

    PS0T = nc.alloc_psum_tensor("PS0", [128, 1024], BF16)
    PS = [PS0T] + [nc.alloc_psum_tensor(f"PS{i}", [128, 512], F32)
                   for i in range(1, 8)]
    PS0b = PS0T.ap()

    def F(i, w=1024, off=0):
        return b_F[:, i * 1032 + off: i * 1032 + off + w]

    def Wtile(slot, i, c0=0, cw=D):
        base = slot * 16 * D + i * D
        return b_W[:, base + c0: base + c0 + cw]

    def hs(j, w, off=0):
        return b_h[:, j * GW0 + off: j * GW0 + off + w]

    def xlnT(dc, off, w):
        return b_xlnT[:, dc * GW0 + off: dc * GW0 + off + w]

    def tvcol(v, dc):
        return b_tv[:, v * DC + dc: v * DC + dc + 1]

    def drv(v, dc):
        return b_drv[:, v * DC + dc: v * DC + dc + 1]

    st = {}

    def ps_wait(idxs):
        for i in (idxs if isinstance(idxs, (list, tuple)) else [idxs]):
            p.wait("T", p.V, st.get(f"psv{i}", 0))
            p.wait("T", p.S, st.get(f"pss{i}", 0))

    def ps_used(i, eng, tok):
        st[f"ps{'v' if eng == 'V' else 's'}{i}"] = tok

    PAIR = {0: (2, 3), 1: (5, 6)}
    TRIPLE = {0: (2, 3, 4), 1: (5, 6, 7)}

    t_cst = 0
    for dst, src in [(b_gbeb[:, :], d_gbeb), (b_mask[:, :], d_mask),
                     (b_id[:, :], d_ident)]:
        t_cst = p.dma("Y", dst, src, rs=p.ct)
    for v in range(5):
        for dc in range(DC):
            t_cst = p.dma("Y", tvcol(v, dc), d_tvec[v, dc], rs=p.ct)
    t_wge = 0
    for dc in range(DC):
        t_wge = p.dma("G", b_wge[:, dc * 10:(dc + 1) * 10],
                      d_wge[dc * 128:(dc + 1) * 128, :], rs=p.wg)
    st["wge"] = t_wge

    p.wait("S", p.ct, t_cst)
    t_drv = 0
    for dc in range(DC):
        p.op("S", lambda e, o=drv(0, dc), i=tvcol(0, dc):
             e.activation(o, i, AF.Exp))
        p.op("S", lambda e, o=drv(0, dc), i=drv(0, dc): e.mul(o, i, -1.0))
        p.op("S", lambda e, o=drv(1, dc), i=drv(0, dc):
             e.activation(o, i, AF.Exp))
        t_drv = p.op("S", lambda e, o=drv(2, dc), i=tvcol(1, dc):
                     e.activation(o, i, AF.Exp), inc=True)
    st["drv"] = t_drv
    p.wait("V", p.S, t_drv)
    p.op("V", lambda e: e.memset(b_ends[:, :], 0.0))
    st["bnd0"] = p.op("V", lambda e: e.memset(b_bnd[:, :], 0.0), inc=True)

    conv_done = {}
    conv_state = {"war": 0}
    conv_pieces = {
        "win": (d_win, s_wb["win"], D, 2 * HID),
        "w1": (d_w1, s_wb["w1"], D, 2 * HID),
        "wout": (d_wout, s_wb["wout"], HID, D),
        "wk": (d_wk, s_wb["wk"], D, D),
        "wv": (d_wv, s_wb["wv"], D, D),
        "wr": (d_wr, s_wb["wr"], D, D),
        "wo": (d_wo, s_wb["wo"], D, D),
    }
    for e_ in range(NE):
        conv_pieces[f"w2_{e_}"] = (d_w2[e_], s_wb[f"w2_{e_}"], HID, D)

    def conv(name):
        if name in conv_done:
            return conv_done[name]
        src, dst, rows, cols = conv_pieces[name]
        tok_y = 0
        for r0 in range(0, rows, 128):
            for c0 in range(0, cols, 1024):
                w = min(1024, cols - c0)
                cb = b_conv[:, 0:w]
                p.wait("G", p.cw, conv_state["war"])
                tg = p.dma("G", cb, src[r0:r0 + 128, c0:c0 + w], rs=p.cr)
                p.wait("Y", p.cr, tg)
                tok_y = p.dma("Y", dst[r0:r0 + 128, c0:c0 + w], cb, rs=p.cw)
                conv_state["war"] = tok_y
        conv_done[name] = tok_y
        return tok_y

    def ln_transpose(sub, xap, parts="full"):
        xlnap = b_xln[:, (sub % 2) * D:(sub % 2 + 1) * D]
        p.wait("V", p.T, st.get(f"xlnfree{sub % 2}", 0))
        par_ = sub % 2
        stt = b_st[:, par_ * 8:par_ * 8 + 8]
        p.wait("V", p.S, st.get(f"lnstatS{par_}", 0))
        p.op("V", lambda e, o=stt[:, 0:1], i=xap: e.reduce_sum(o, i, axis=AX.X))
        p.wait("S", p.T, st.get("winfree0", 0))
        tok_s = p.op("S", lambda e, o=b_winf[:, 0:D], i=xap,
                     a=stt[:, 1:2]:
                     e.activation(o, i, AF.Square, accum_out=a), inc=True)
        st["sqtok"] = tok_s
        tok_m = p.op("V", lambda e, o=stt[:, 2:3], i=stt[:, 0:1]:
                     e.tensor_scalar_mul(o, i, 1.0 / D), inc=True)
        p.wait("S", p.V, tok_m)
        tok_s2 = p.op("S", lambda e, o=stt[:, 3:4], i=stt[:, 2:3]:
                      e.activation(o, i, AF.Square), inc=True)
        p.wait("V", p.S, tok_s)
        p.op("V", lambda e, o=stt[:, 4:5], i=stt[:, 1:2]:
             e.tensor_scalar_mul(o, i, 1.0 / D))
        p.wait("V", p.S, tok_s2)
        p.op("V", lambda e, o=stt[:, 4:5], a=stt[:, 4:5], b=stt[:, 3:4]:
             e.tensor_sub(o, a, b))
        tok_v = p.op("V", lambda e, o=stt[:, 4:5], i=stt[:, 4:5]:
                     e.tensor_scalar_add(o, i, 1e-5), inc=True)
        p.wait("S", p.V, tok_v)
        tok_s = p.op("S", lambda e, o=stt[:, 5:6], i=stt[:, 4:5]:
                     e.activation(o, i, AF.Sqrt), inc=True)
        st[f"lnstatS{par_}"] = tok_s
        p.wait("V", p.S, tok_s)
        p.op("V", lambda e, o=stt[:, 6:7], i=stt[:, 5:6]: e.reciprocal(o, i))
        ln_tok = p.op("V", lambda e, o=xlnap, i=xap, m=stt[:, 2:3],
                      r=stt[:, 6:7]:
                      e.tensor_scalar(o, i, m, r, ALU.subtract, ALU.mult),
                      inc=True)
        if stage < 0.18 or parts == "ln":
            return (ln_tok, 1)
        p.wait("T", p.V, ln_tok)
        ps_wait(0)
        ttok = None
        for dc in range(DC):
            ttok = p.op("T", lambda e, o=PS0b[:, dc * 128:(dc + 1) * 128],
                        i=xlnap[:, dc * 128:(dc + 1) * 128], idn=b_id[:, :]:
                        e.transpose(o, i, idn), inc=(dc == DC - 1))
        st[f"xlnfree{sub % 2}"] = ttok
        if stage < 0.22 or parts == "tr":
            return (ttok, 1)
        vt = stk = None
        for dc in range(DC):
            s = "V"
            p.wait(s, p.T, ttok)
            if s == "V":
                tok = p.op(s, lambda e, o=xlnT(dc, sub * 128, 128),
                           i=PS0b[:, dc * 128:(dc + 1) * 128]:
                           e.tensor_copy(o, i), inc=True)
            else:
                tok = p.op(s, lambda e, o=xlnT(dc, sub * 128, 128),
                           i=PS0b[:, dc * 128:(dc + 1) * 128]:
                           e.copy(o, i), inc=True)
            if s == "V":
                vt = tok
            else:
                stk = tok
        ps_used(0, "V", vt)
        ps_used(0, "S", stk)
        return vt, stk

    def ffn_core(pass_id, g, x_resid_src, store_dst, tr_toks,
                 store_q="Y", epi=None):
        if stage < 0.5:
            return []
        nsub = 9 if (g == 0 and pass_id == "p1") else 8
        w = nsub * 128
        tsp = ([(0, 128), (128, 512), (640, 512)] if w == GW0
               else [(0, 512), (512, 512)])
        first = pass_id == "p1" and g == 0

        for sub in range(nsub):
            vt, stk = tr_toks[sub]
            p.wait("T", p.V, vt)
            p.wait("T", p.S, stk)
            if sub == 0:
                p.wait("T", p.wg, st["wge"])
            p.wait("T", p.V, st.get("gatefree", 0))
            ps = PS[1][:, 0:10]
            gtok = None
            for dc in range(DC):
                gtok = p.op("T", lambda e, o=ps,
                            l=xlnT(dc, sub * 128, 128),
                            r=b_wge[:, dc * 10:(dc + 1) * 10],
                            s0=(dc == 0), s1=(dc == DC - 1):
                            e.matmul(o, l, r, start=s0, stop=s1),
                            inc=(dc == DC - 1))
            p.wait("V", p.T, gtok)
            tok = p.op("V", lambda e, o=b_rt[:, sub * 64: sub * 64 + 10],
                       a=ps, b=b_gbeb[:, :]: e.tensor_add(o, a, b), inc=True)
            st["gatefree"] = tok
        ps_used(1, "V", st["gatefree"])

        def rtc(sub, c0, c1=None):
            c1 = c0 + 1 if c1 is None else c1
            return b_rt[:, sub * 64 + c0: sub * 64 + c1]

        vtok = None
        for sub in range(nsub):
            p.op("V", lambda e, o=rtc(sub, 10), a=rtc(sub, 0), b=rtc(sub, 1):
                 e.tensor_sub(o, a, b))
            p.op("V", lambda e, o=rtc(sub, 11), i=rtc(sub, 10):
                 e.tensor_single_scalar(o, i, 0.0, ALU.is_ge))
            p.op("V", lambda e, o=rtc(sub, 12), i=rtc(sub, 2, 6):
                 e.reduce_max(o, i, axis=AX.X))
            p.op("V", lambda e, o=rtc(sub, 13), i=rtc(sub, 6, 10):
                 e.reduce_max(o, i, axis=AX.X))
            p.op("V", lambda e, o=rtc(sub, 12), i=rtc(sub, 12):
                 e.tensor_scalar_mul(o, i, -1.0))
            vtok = p.op("V", lambda e, o=rtc(sub, 13), i=rtc(sub, 13):
                        e.tensor_scalar_mul(o, i, -1.0), inc=True)
        p.wait("S", p.V, vtok)
        stok = None
        for sub in range(nsub):
            p.op("S", lambda e, o=rtc(sub, 14), i=rtc(sub, 10):
                 e.activation(o, i, AF.Abs))
            p.op("S", lambda e, o=rtc(sub, 15), i=rtc(sub, 14):
                 e.activation(o, i, AF.Sigmoid))
        for sub in range(nsub):
            p.op("S", lambda e, o=rtc(sub, 18, 22), i=rtc(sub, 2, 6),
                 b=rtc(sub, 12), a=rtc(sub, 26):
                 e.activation(o, i, AF.Exp, bias=b, accum_out=a))
            stok = p.op("S", lambda e, o=rtc(sub, 22, 26), i=rtc(sub, 6, 10),
                        b=rtc(sub, 13), a=rtc(sub, 27):
                        e.activation(o, i, AF.Exp, bias=b, accum_out=a),
                        inc=True)
        p.wait("V", p.S, stok)
        wse_tok = None
        for sub in range(nsub):
            p.op("V", lambda e, o=rtc(sub, 28), i=rtc(sub, 26):
                 e.reciprocal(o, i))
            p.op("V", lambda e, o=rtc(sub, 29), i=rtc(sub, 27):
                 e.reciprocal(o, i))
            p.op("V", lambda e, o=rtc(sub, 16), a=rtc(sub, 15), b=rtc(sub, 11):
                 e.tensor_mul(o, a, b))
            p.op("V", lambda e, o=rtc(sub, 17), a=rtc(sub, 15), b=rtc(sub, 16):
                 e.tensor_sub(o, a, b))
            p.op("V", lambda e, o=rtc(sub, 30), a=rtc(sub, 16), b=rtc(sub, 28):
                 e.tensor_mul(o, a, b))
            p.op("V", lambda e, o=rtc(sub, 31), a=rtc(sub, 17), b=rtc(sub, 29):
                 e.tensor_mul(o, a, b))
            p.op("V", lambda e, o=rtc(sub, 32, 36), i=rtc(sub, 18, 22),
                 s=rtc(sub, 30): e.tensor_scalar_mul(o, i, s))
            p.op("V", lambda e, o=rtc(sub, 36, 40), i=rtc(sub, 22, 26),
                 s=rtc(sub, 31): e.tensor_scalar_mul(o, i, s))
            p.op("V", lambda e, o=rtc(sub, 40), i=rtc(sub, 32, 40):
                 e.reduce_max(o, i, axis=AX.X))
            p.op("V", lambda e, o=rtc(sub, 41, 49), i=rtc(sub, 32, 40),
                 s=rtc(sub, 40): e.tensor_single_scalar(o, i, s, ALU.is_ge))
            p.op("V", lambda e, o=rtc(sub, 49, 57), i=rtc(sub, 41, 49),
                 b=rtc(sub, 32, 40):
                 e.scalar_tensor_tensor(o, i, -2.0, b, ALU.mult, ALU.add))
            p.op("V", lambda e, o=rtc(sub, 57), i=rtc(sub, 49, 57):
                 e.reduce_max(o, i, axis=AX.X))
            p.op("V", lambda e, o=rtc(sub, 41, 49), i=rtc(sub, 32, 40),
                 s=rtc(sub, 57): e.tensor_single_scalar(o, i, s, ALU.is_ge))
            p.op("V", lambda e, o=rtc(sub, 49, 57), a=rtc(sub, 41, 49),
                 b=rtc(sub, 32, 40): e.tensor_mul(o, a, b))
            wse_tok = p.op("V", lambda e, o=b_wse[:, sub * 8:(sub + 1) * 8],
                           i=rtc(sub, 49, 57):
                           e.tensor_scalar_mul(o, i, 0.5), inc=True)

        if stage < 0.6:
            return []
        jblk_tok = [None] * 32

        def emit_jblk(k):
            if k >= 32 or jblk_tok[k] is not None:
                return
            wname = "win" if k < 16 else "w1"
            jb = k % 16
            slot = k % 2
            base = slot * DC * 256
            if first:
                conv(wname)
                p.wait("Y", p.cw, conv_done[wname])
            p.wait("Y", p.T, st.get(f"winfree{slot}", 0))
            if slot == 0:
                p.wait("Y", p.S, st.get("sqtok", 0))
            ws = p.winS0 if slot == 0 else p.winS1
            tok = None
            for dc in range(DC):
                tok = p.dma(
                    "Y", b_win[:, base + dc * 256: base + (dc + 1) * 256],
                    s_wb[wname][dc * 128:(dc + 1) * 128,
                                jb * 256:(jb + 1) * 256], rs=ws)
            jblk_tok[k] = (ws, tok)

        emit_jblk(0)
        emit_jblk(1)

        def load_W(slot, wname, war_tok):
            if first:
                cv = conv(wname)
                p.wait("S", p.cw, cv)
            p.wait("S", p.T, war_tok)
            wq = p.wq0 if slot == 0 else p.wq1
            tok = None
            for i in range(16):
                tok = p.dma("S", Wtile(slot, i),
                            s_wb[wname][i * 128:(i + 1) * 128, :], rs=wq)
            return (wq, tok)

        wout_tok = load_W(0, "wout", st.get("Wfree0", 0))
        w2_tok = {0: load_W(1, "w2_0", st.get("Wfree1", 0))}

        def h_prod(m_i):
            silu_toks = [None] * NJC
            last_v = None
            for jc in range(2 * NJC):
                k = m_i * 16 + jc // 2
                slot = k % 2
                base = slot * DC * 256 + (jc % 2) * 128
                p.waitt("T", jblk_tok[k])
                if jc == 0:
                    p.wait("T", p.xs, st.get("hfreeY", 0))
                is_a = jc < NJC
                hj = jc if is_a else jc - NJC
                par = jc % 2
                tiles = TRIPLE[par][:len(tsp)]
                ps_wait(list(tiles))
                mms = []
                for ci, (t0, tn) in enumerate(tsp):
                    mm = None
                    for dc in range(DC):
                        mm = p.op(
                            "T",
                            lambda e, o=PS[tiles[ci]][:, 0:tn],
                            l=b_win[:, base + dc * 256: base + dc * 256 + 128],
                            r=xlnT(dc, t0, tn),
                            s0=(dc == 0), s1=(dc == DC - 1):
                            e.matmul(o, l, r, start=s0, stop=s1),
                            inc=(dc == DC - 1))
                    mms.append(mm)
                if jc % 2 == 1:
                    st[f"winfree{slot}"] = mms[-1]
                    emit_jblk(k + 2)
                if is_a:
                    tok = None
                    for ci, (t0, tn) in enumerate(tsp):
                        p.wait("S", p.T, mms[ci])
                        tok = p.op("S", lambda e, o=hs(hj, tn, t0),
                                   i=PS[tiles[ci]][:, 0:tn]:
                                   e.activation(o, i, AF.Silu), inc=True)
                        ps_used(tiles[ci], "S", tok)
                    silu_toks[hj] = tok
                else:
                    p.wait("V", p.S, silu_toks[hj])
                    for ci, (t0, tn) in enumerate(tsp):
                        p.wait("V", p.T, mms[ci])
                        tok = p.op("V", lambda e, o=hs(hj, tn, t0),
                                   a=hs(hj, tn, t0), b=PS[tiles[ci]][:, 0:tn]:
                                   e.tensor_mul(o, a, b), inc=True)
                        ps_used(tiles[ci], "V", tok)
                    last_v = tok
            return last_v

        if pass_id == "p4":
            p.wait("S", p.V, st.get("fin", 0))
        if stage < 0.7:
            return []
        h1_v = h_prod(0)
        if debug and pass_id == "p1" and g == 0:
            p.wait("G", p.V, h1_v)
            p.dma("G", dbg_xlnT[:, :], b_xlnT[:, 0:GW0])
            p.dma("G", dbg_h[:, :], b_h[:, 0:GW0])
            p.dma("G", dbg_wse[:, :], b_wse[:, :])
            p.dma("G", dbg_rt[:, :], b_rt[:, :])
            dtok = p.dma("G", dbg_w[:, :], s_wb["win"][0:128, :])
            p.wait("V", p.G, dtok)
            p.wait("S", p.G, dtok)
            p.wait("T", p.G, dtok)
        if stage < 0.8:
            return []

        p.waitt("T", wout_tok)
        p.wait("T", p.V, h1_v)
        if pass_id == "p4":
            p.wait("V", p.T, st.get("xr_T_done", 0))
        for sub in range(nsub):
            par = sub % 2
            pr = PAIR[par]
            ps_wait(list(pr))
            mms = [None, None]
            for jc in range(NJC):
                for eh in range(2):
                    mms[eh] = p.op(
                        "T",
                        lambda e, o=PS[pr[eh]][:, :],
                        l=hs(jc, 128, sub * 128),
                        r=Wtile(0, jc, eh * 512, 512),
                        s0=(jc == 0), s1=(jc == NJC - 1):
                        e.matmul(o, l, r, start=s0, stop=s1),
                        inc=(jc == NJC - 1))
            for eh in range(2):
                p.wait("V", p.T, mms[eh])
                tok = p.op("V", lambda e,
                           o=b_acc[:, sub * D + eh * 512:
                                   sub * D + eh * 512 + 512],
                           i=PS[pr[eh]][:, :]:
                           e.tensor_scalar_mul(o, i, 0.5), inc=True)
                ps_used(pr[eh], "V", tok)
        wout_T_done = p.T.n
        if debug and pass_id == "p1" and g == 0:
            p.wait("G", p.V, p.V.n)
            dtok = p.dma("G", dbg_acc[:, :], b_acc[:, 0:D])
            p.wait("V", p.G, dtok)
            p.wait("T", p.G, dtok)
        if stage < 0.85:
            return []

        hh_v = h_prod(1)

        if stage < 0.9:
            return []
        p.wait("T", p.V, hh_v)
        p.wait("T", p.V, wse_tok)
        w2_T_done = {}
        for e_ in range(NE):
            slot = 1 - (e_ % 2)
            p.waitt("T", w2_tok[e_])
            if e_ + 1 < NE:
                war = wout_T_done if e_ == 0 else w2_T_done[e_ - 1]
                w2_tok[e_ + 1] = load_W(1 - slot, f"w2_{e_ + 1}", war)
            for sub in range(nsub):
                par = sub % 2
                pr = PAIR[par]
                ps_wait(list(pr))
                mms = [None, None]
                for jc in range(NJC):
                    for eh in range(2):
                        mms[eh] = p.op(
                            "T",
                            lambda e, o=PS[pr[eh]][:, :],
                            l=hs(jc, 128, sub * 128),
                            r=Wtile(slot, jc, eh * 512, 512),
                            s0=(jc == 0), s1=(jc == NJC - 1):
                            e.matmul(o, l, r, start=s0, stop=s1),
                            inc=(jc == NJC - 1))
                for eh in range(2):
                    p.wait("V", p.T, mms[eh])
                    tok = p.op("V", lambda e,
                               o=b_acc[:, sub * D + eh * 512:
                                       sub * D + eh * 512 + 512],
                               i=PS[pr[eh]][:, :],
                               s=b_wse[:, sub * 8 + e_: sub * 8 + e_ + 1],
                               a=b_acc[:, sub * D + eh * 512:
                                       sub * D + eh * 512 + 512]:
                               e.scalar_tensor_tensor(o, i, s, a, ALU.mult,
                                                      ALU.add), inc=True)
                    ps_used(pr[eh], "V", tok)
            w2_T_done[e_] = p.T.n
        st["hfree"] = p.T.n
        st["Wfree0"] = p.T.n
        st["Wfree1"] = p.T.n

        if stage < 0.95:
            return []
        res = []
        for sub in range(nsub):
            skip = (pass_id == "p1" and g == 0 and sub == 0)
            slot = b_x[:, (sub % 2) * D:(sub % 2 + 1) * D]
            p.wait("Y", p.V, st.get(f"xresfree{sub % 2}", 0))
            p.wait("Y", p.S, st.get(f"xresfreeS{sub % 2}", 0))
            p.waitt("Y", st.get(f"xresfreeG{sub % 2}"))
            p.waitt("Y", st.get(f"xstore{sub % 2}"))
            xl = p.xlA if sub % 2 == 0 else p.xlB
            ytok = p.dma("Y", slot, x_resid_src(sub), rs=xl)
            p.wait("V", xl, ytok)
            tok = p.op("V", lambda e, o=slot, a=slot,
                       b=b_acc[:, sub * D:(sub + 1) * D]:
                       e.tensor_add(o, a, b), inc=True)
            res.append((tok, slot, skip))
            dst = store_dst(sub)
            if dst is not None and not skip:
                p.wait(store_q, p.V, tok)
                if store_q == "G":
                    so = p.oA if sub % 2 == 0 else p.oB
                else:
                    so = p.stA if sub % 2 == 0 else p.stB
                stok = p.dma(store_q, dst, slot, rs=so)
                if store_q == "G":
                    st[f"xresfreeG{sub % 2}"] = (so, stok)
                else:
                    st[f"xstore{sub % 2}"] = (so, stok)
            if epi is not None:
                epi(sub, tok, slot)
            else:
                st[f"xresfree{sub % 2}"] = tok
        st["accfree"] = res[-1][0]
        return res

    def p1_tail(g, tr2):
        nsub = 9 if g == 0 else 8
        w = nsub * 128
        own0 = 128 if g == 0 else 0
        ln2s = max((t[1] or 0) for t in tr2)
        if stage < 1.2:
            return

        if g == 0:
            conv("wk")
            cv = conv("wv")
            p.wait("Y", p.cw, cv)
        p.wait("Y", p.T, st["Wfree0"])
        wk_tok = None
        for dc in range(DC):
            wk_tok = p.dma("Y", Wtile(0, dc),
                           s_wb["wk"][dc * 128:(dc + 1) * 128, :], rs=p.aw)
        for dc in range(DC):
            wk_tok = p.dma("Y", Wtile(0, 8 + dc),
                           s_wb["wv"][dc * 128:(dc + 1) * 128, :], rs=p.aw)

        p.wait("V", p.S, ln2s)
        p.wait("V", p.T, st["hfree"])
        p.wait("V", p.xs, st.get("hfreeY", 0))
        mix_last = None
        for mi, tv in enumerate((2, 3)):
            for dc in range(DC):
                M0 = mi * 8 + dc
                p.op("V", lambda e, o=b_st[:, 16:17], i=tvcol(tv, dc):
                     e.tensor_scalar(o, i, -1.0, 1.0, ALU.mult, ALU.add))
                p.op("V", lambda e, o=hs(M0, w), i=xlnT(dc, 0, w),
                     s=tvcol(tv, dc): e.tensor_scalar_mul(o, i, s))
                tok = p.op("V", lambda e, o=hs(M0, w - 1, 1),
                           i=xlnT(dc, 0, w - 1), s=b_st[:, 16:17],
                           a=hs(M0, w - 1, 1):
                           e.scalar_tensor_tensor(o, i, s, a, ALU.mult,
                                                  ALU.add), inc=True)
                if g > 0:
                    tok = p.op("V", lambda e, o=hs(M0, 1, 0),
                               i=b_bnd[:, dc:dc + 1], s=b_st[:, 16:17],
                               a=hs(M0, 1, 0):
                               e.scalar_tensor_tensor(o, i, s, a, ALU.mult,
                                                      ALU.add), inc=True)
                mix_last = tok
        if stage < 1.25:
            return
        p.wait("T", p.V, mix_last)
        p.wait("T", p.aw, wk_tok)
        for ec in range(DC):
            ps_wait([2, 3])
            mmk = [None, None]
            for th in range(2):
                for dc in range(DC):
                    mmk[th] = p.op(
                        "T",
                        lambda e, o=PS[2 + th][:, :],
                        l=Wtile(0, dc, ec * 128, 128),
                        r=hs(dc, 512, own0 + th * 512),
                        s0=(dc == 0), s1=(dc == DC - 1):
                        e.matmul(o, l, r, start=s0, stop=s1),
                        inc=(dc == DC - 1))
            ps_wait([5, 6])
            mmv = [None, None]
            for th in range(2):
                for dc in range(DC):
                    mmv[th] = p.op(
                        "T",
                        lambda e, o=PS[5 + th][:, :],
                        l=Wtile(0, 8 + dc, ec * 128, 128),
                        r=hs(8 + dc, 512, own0 + th * 512),
                        s0=(dc == 0), s1=(dc == DC - 1):
                        e.matmul(o, l, r, start=s0, stop=s1),
                        inc=(dc == DC - 1))
            ekslot, vslot = ec % 2, 2 + ec % 2
            dump = b_acc[:, 8 * D:9 * D]
            p.waitt("S", st.get(f"spillek{ekslot}"))
            ektok = None
            for th in range(2):
                p.wait("S", p.T, mmk[th])
                ektok = p.op("S", lambda e, o=F(ekslot, 512, th * 512),
                             i=PS[2 + th][:, :]:
                             e.activation(o, i, AF.Exp), inc=True)
                ps_used(2 + th, "S", ektok)
            p.waitt("V", st.get(f"spillekv{vslot}"))
            vtok = None
            for th in range(2):
                p.wait("V", p.T, mmv[th])
                vtok = p.op("V", lambda e, o=F(vslot, 512, th * 512),
                            i=PS[5 + th][:, :]: e.tensor_copy(o, i), inc=True)
                ps_used(5 + th, "V", vtok)
            p.wait("V", p.S, ektok)
            ekvtok = p.op("V", lambda e, o=F(vslot), a=F(ekslot), b=F(vslot):
                          e.tensor_mul(o, a, b), inc=True)
            if stage < 1.35:
                continue
            p.op("V", lambda e, o=dump,
                 d0=drv(1, ec).broadcast_to((128, GT)), d1=F(vslot),
                 i0=b_ends[:, ec:ec + 1]:
                 e.tensor_tensor_scan(o, d0, d1, i0, ALU.mult, ALU.add))
            p.op("V", lambda e, o=b_ends[:, ec:ec + 1],
                 i=dump[:, GT - 1:GT]: e.tensor_copy(o, i))
            sbk = p.op("V", lambda e, o=dump,
                       d0=drv(1, ec).broadcast_to((128, GT)), d1=F(ekslot),
                       i0=b_ends[:, 8 + ec:8 + ec + 1]:
                       e.tensor_tensor_scan(o, d0, d1, i0, ALU.mult, ALU.add),
                       inc=True)
            endtok = p.op("V", lambda e, o=b_ends[:, 8 + ec:8 + ec + 1],
                          i=dump[:, GT - 1:GT]: e.tensor_copy(o, i),
                          inc=True)
            st["endtok"] = endtok
            if stage < 1.45:
                continue
            p.wait("G", p.V, ekvtok)
            spv = getattr(p, f"sp{vslot}")
            st[f"spillekv{vslot}"] = (spv, p.dma(
                "G", s_ekv[ec][:, g * GT:(g + 1) * GT], F(vslot), rs=spv))
            p.wait("G", p.V, sbk)
            spk = getattr(p, f"sp{ekslot}")
            st[f"spillek{ekslot}"] = (spk, p.dma(
                "G", s_ek[ec][:, g * GT:(g + 1) * GT], F(ekslot), rs=spk))
        proj_T = p.T.n
        if stage < 1.49:
            return
        p.wait("V", p.T, proj_T)
        mixr_last = None
        for dc in range(DC):
            p.op("V", lambda e, o=b_st[:, 16:17], i=tvcol(4, dc):
                 e.tensor_scalar(o, i, -1.0, 1.0, ALU.mult, ALU.add))
            p.op("V", lambda e, o=hs(dc, w), i=xlnT(dc, 0, w),
                 s=tvcol(4, dc): e.tensor_scalar_mul(o, i, s))
            tok = p.op("V", lambda e, o=hs(dc, w - 1, 1),
                       i=xlnT(dc, 0, w - 1), s=b_st[:, 16:17],
                       a=hs(dc, w - 1, 1):
                       e.scalar_tensor_tensor(o, i, s, a, ALU.mult,
                                              ALU.add), inc=True)
            if g > 0:
                tok = p.op("V", lambda e, o=hs(dc, 1, 0),
                           i=b_bnd[:, dc:dc + 1], s=b_st[:, 16:17],
                           a=hs(dc, 1, 0):
                           e.scalar_tensor_tensor(o, i, s, a, ALU.mult,
                                                  ALU.add), inc=True)
            mixr_last = tok
        if stage < 1.493:
            return
        for dc in range(DC):
            p.op("V", lambda e, o=b_bnd[:, dc:dc + 1],
                 i=xlnT(dc, w - 1, 1): e.tensor_copy(o, i))
        if stage < 1.497:
            return
        hfY = None
        for dc in range(DC):
            p.wait("Y", p.V, mixr_last if dc == 0 else 0)
            hfY = p.dma("Y", s_xr[dc][:, g * GT:(g + 1) * GT],
                        hs(dc, GT, own0), rs=p.xs)
        st["hfreeY"] = hfY
        st["hfree"] = p.T.n
        st["Wfree0"] = p.T.n

    def p2c_group(g):
        nsub = 8
        if g == 0:
            for sem in (p.xs, p.sp0, p.sp1, p.sp2, p.sp3):
                p.wait("Y", sem, sem.n)

        if g == 0:
            conv("wr")
            cv = conv("wo")
            p.wait("Y", p.cw, cv)
        p.wait("Y", p.T, st["Wfree0"])
        wr_tok = None
        for dc in range(DC):
            wr_tok = p.dma("Y", Wtile(0, dc),
                           s_wb["wr"][dc * 128:(dc + 1) * 128, :], rs=p.aw)
        for dc in range(DC):
            wr_tok = p.dma("Y", Wtile(0, 8 + dc),
                           s_wb["wo"][dc * 128:(dc + 1) * 128, :], rs=p.aw)
        p.wait("Y", p.V, st.get("accfree", 0))
        xr_tok = None
        for dc in range(DC):
            xr_tok = p.dma("Y", b_accb[:, dc * D:(dc + 1) * D],
                           s_xr[dc][:, g * GT:(g + 1) * GT], rs=p.xrl)

        p.wait("T", p.aw, wr_tok)
        p.wait("T", p.xrl, xr_tok)
        p.wait("T", p.T, st["hfree"])
        rs_tok = None
        for ec in range(DC):
            ps_wait([2, 3])
            mm = [None, None]
            for th in range(2):
                for dc in range(DC):
                    mm[th] = p.op(
                        "T",
                        lambda e, o=PS[2 + th][:, :],
                        l=Wtile(0, dc, ec * 128, 128),
                        r=b_accb[:, dc * D + th * 512: dc * D + th * 512 + 512],
                        s0=(dc == 0), s1=(dc == DC - 1):
                        e.matmul(o, l, r, start=s0, stop=s1),
                        inc=(dc == DC - 1))
            for th in range(2):
                p.wait("S", p.T, mm[th])
                rs_tok = p.op("S", lambda e, o=hs(8 + ec, 512, th * 512),
                              i=PS[2 + th][:, :]:
                              e.activation(o, i, AF.Sigmoid), inc=True)
                ps_used(2 + th, "S", rs_tok)
        st["xr_T_done"] = p.T.n

        fin_tok = None
        for dc in range(DC):
            bslot = dc % 2
            p.wait("Y", p.V, st.get(f"bflfree{bslot}", 0))
            bf = p.bflA if bslot == 0 else p.bflB
            p.dma("Y", b_bfl[:, bslot * 2 * D: bslot * 2 * D + D],
                  s_ekv[dc][:, g * GT:(g + 1) * GT], rs=bf)
            ek_l = p.dma("Y", b_bfl[:, bslot * 2 * D + D: (bslot + 1) * 2 * D],
                         s_ek[dc][:, g * GT:(g + 1) * GT], rs=bf)
            ekvap = b_bfl[:, bslot * 2 * D: bslot * 2 * D + D]
            ekap = b_bfl[:, bslot * 2 * D + D: (bslot + 1) * 2 * D]
            p.wait("V", bf, ek_l)
            p.op("V", lambda e, o=F(0, 1, 0), i=b_cin[:, dc:dc + 1]:
                 e.tensor_copy(o, i))
            p.op("V", lambda e, o=F(0, GT, 1),
                 d0=drv(1, dc).broadcast_to((128, GT)), d1=ekvap,
                 i0=b_cin[:, dc:dc + 1]:
                 e.tensor_tensor_scan(o, d0, d1, i0, ALU.mult, ALU.add))
            p.op("V", lambda e, o=b_cin[:, dc:dc + 1], i=F(0, 1, GT):
                 e.tensor_copy(o, i))
            p.op("V", lambda e, o=F(1, 1, 0), i=b_cin[:, 8 + dc:8 + dc + 1]:
                 e.tensor_copy(o, i))
            p.op("V", lambda e, o=F(1, GT, 1),
                 d0=drv(1, dc).broadcast_to((128, GT)), d1=ekap,
                 i0=b_cin[:, 8 + dc:8 + dc + 1]:
                 e.tensor_tensor_scan(o, d0, d1, i0, ALU.mult, ALU.add))
            p.op("V", lambda e, o=b_cin[:, 8 + dc:8 + dc + 1], i=F(1, 1, GT):
                 e.tensor_copy(o, i))
            p.op("V", lambda e, o=F(2), i=ekvap, s=drv(2, dc), a=F(0, GT, 0):
                 e.scalar_tensor_tensor(o, i, s, a, ALU.mult, ALU.add))
            dtok = p.op("V", lambda e, o=F(3), i=ekap, s=drv(2, dc),
                        a=F(1, GT, 0):
                        e.scalar_tensor_tensor(o, i, s, a, ALU.mult, ALU.add),
                        inc=True)
            st[f"bflfree{bslot}"] = dtok
            vtok = p.op("V", lambda e, o=F(4), i=F(3):
                        e.tensor_single_scalar(o, i, 1e-6, ALU.max), inc=True)
            p.wait("S", p.V, vtok)
            sln = p.op("S", lambda e, o=F(3), i=F(4):
                       e.activation(o, i, AF.Ln), inc=True)
            p.wait("S", p.S, sln)
            srd = p.op("S", lambda e, o=F(4), i=F(3):
                       e.activation(o, i, AF.Exp, scale=-1.0), inc=True)
            p.wait("V", p.S, srd)
            p.op("V", lambda e, o=F(2), a=F(2), b=F(4): e.tensor_mul(o, a, b))
            p.wait("V", p.S, rs_tok if dc == 0 else 0)
            fin_tok = p.op("V", lambda e, o=hs(dc, GT), a=F(2),
                           b=hs(8 + dc, GT): e.tensor_mul(o, a, b), inc=True)
        st["fin"] = fin_tok

        p.wait("T", p.V, fin_tok)
        tr3 = []
        for sub in range(nsub):
            par = sub % 2
            pr = PAIR[par]
            ps_wait(list(pr))
            mm = [None, None]
            for dc in range(DC):
                for eh in range(2):
                    mm[eh] = p.op(
                        "T",
                        lambda e, o=PS[pr[eh]][:, :],
                        l=hs(dc, 128, sub * 128),
                        r=Wtile(0, 8 + dc, eh * 512, 512),
                        s0=(dc == 0), s1=(dc == DC - 1):
                        e.matmul(o, l, r, start=s0, stop=s1),
                        inc=(dc == DC - 1))
            slot = b_x[:, (sub % 2) * D:(sub % 2 + 1) * D]
            row = g * GT + sub * 128
            p.wait("Y", p.V, st.get(f"xresfree{sub % 2}", 0))
            p.wait("Y", p.S, st.get(f"xresfreeS{sub % 2}", 0))
            p.waitt("Y", st.get(f"xresfreeG{sub % 2}"))
            p.waitt("Y", st.get(f"xstore{sub % 2}"))
            xl = p.xlA if sub % 2 == 0 else p.xlB
            ytok = p.dma("Y", slot, s_x1[row:row + 128, :], rs=xl)
            p.wait("V", xl, ytok)
            tok = None
            for eh in range(2):
                p.wait("V", p.T, mm[eh])
                tok = p.op("V", lambda e, o=slot[:, eh * 512:(eh + 1) * 512],
                           a=slot[:, eh * 512:(eh + 1) * 512],
                           b=PS[pr[eh]][:, :]: e.tensor_add(o, a, b), inc=True)
                ps_used(pr[eh], "V", tok)
            p.wait("Y", p.V, tok)
            p.dma("Y", s_x2[row:row + 128, :], slot)
            tr3.append(ln_transpose(sub, slot))
            st[f"xresfree{sub % 2}"] = p.V.n
            st[f"xresfreeS{sub % 2}"] = p.S.n
        st["Wfree0"] = p.T.n
        st["hfree"] = p.T.n
        return tr3

    def xin_row(g, sub):
        return g * GT + sub * 128 + (0 if g == 0 else 128)

    def own_row(g, sub):
        return g * GT + sub * 128 - (128 if g == 0 else 0)

    for g in range([0, NG if stage >= 2 else 1][stage >= 0.12]):
        nsub = 9 if g == 0 else 8
        if stage < 0.3:
            nsub = 1
        tr = []
        for sub in range(nsub):
            slot = b_x[:, (sub % 2) * D:(sub % 2 + 1) * D]
            p.wait("Y", p.V, st.get(f"xresfree{sub % 2}", 0))
            p.wait("Y", p.S, st.get(f"xresfreeS{sub % 2}", 0))
            p.waitt("Y", st.get(f"xstore{sub % 2}"))
            xl = p.xlA if sub % 2 == 0 else p.xlB
            ytok = p.dma("Y", slot,
                         d_xin[xin_row(g, sub): xin_row(g, sub) + 128, :],
                         rs=xl)
            p.wait("V", xl, ytok)
            p.wait("S", xl, ytok)
            if sub == 0 and g == 0:
                p.wait("T", p.V, st["bnd0"])
            tr.append(ln_transpose(sub, slot))
            st[f"xresfree{sub % 2}"] = p.V.n
            st[f"xresfreeS{sub % 2}"] = p.S.n

        tr2 = []

        def epi_p1(sub, tok, slot):
            if stage < 1.1:
                st[f"xresfree{sub % 2}"] = tok
                return
            p.wait("S", p.V, tok)
            if stage >= 1.145:
                parts = "full"
            elif stage >= 1.1405:
                nfull = {1.141: 1, 1.142: 2, 1.143: 4, 1.144: 6}.get(stage, 0)
                parts = "full" if sub < nfull else "tr"
            elif stage >= 1.14:
                parts = "tr"
            else:
                parts = "ln"
            tr2.append(ln_transpose(sub, slot, parts=parts))
            st[f"xresfree{sub % 2}"] = p.V.n
            st[f"xresfreeS{sub % 2}"] = p.S.n

        res = ffn_core(
            "p1", g,
            lambda sub, g=g: d_xin[xin_row(g, sub): xin_row(g, sub) + 128, :],
            lambda sub, g=g: None if (g == 0 and sub == 0)
            else s_x1[own_row(g, sub): own_row(g, sub) + 128, :],
            tr, epi=epi_p1)
        if stage >= 1.1 and res:
            p1_tail(g, tr2)

    if stage >= 3:
        stg = p.op("V", lambda e, o=b_stage[:, :], i=b_ends[:, :],
                   s=b_mask[:, 0:1]: e.tensor_scalar_mul(o, i, s), inc=True)
        p.wait("G", p.V, stg)
        gtok = p.dma("G", cc_in[:, :], b_stage[:, :], rs=p.cc2)
        p.wait("G", p.cc2, gtok)
        p.CC.n += 1
        p.st["G"].append(
            lambda e: e.collective_compute(
                "AllReduce", ALU.add,
                replica_groups=[[0, 1], [2, 3], [4, 5], [6, 7]],
                ins=[cc_in.ap().opt()], outs=[cc_out.ap().opt()],
            ).then_inc(p.CC.h, 1))
        p.wait("G", p.CC, p.CC.n)
        cin_tok = p.dma("G", b_cin[:, :], cc_out[:, :], rs=p.cc2)
        p.wait("V", p.cc2, cin_tok)
        cin_v = p.op("V", lambda e, o=b_cin[:, :], i=b_cin[:, :],
                     s=b_mask[:, 1:2]: e.tensor_scalar_mul(o, i, s), inc=True)
        st["cin"] = cin_v

    if stage >= 4:
        for g in range(NG if stage >= 5 else 1):
            tr3 = p2c_group(g)
            res = ffn_core(
                "p4", g,
                lambda sub, g=g: s_x2[g * GT + sub * 128:
                                      g * GT + sub * 128 + 128, :],
                lambda sub, g=g: d_y[g * GT + sub * 128:
                                     g * GT + sub * 128 + 128, :],
                tr3, store_q="G")
            st["out_last"] = 1

    if debug:
        for r0 in range(0, TOWN, 1024):
            p.dma("G", dbg_x1[r0:r0 + 1024, :], s_x1[r0:r0 + 1024, :])
            p.dma("G", dbg_x2[r0:r0 + 1024, :], s_x2[r0:r0 + 1024, :])
        p.dma("G", dbg_sm[:, 0:16], b_ends[:, :])
        p.dma("G", dbg_sm[:, 16:32], b_cin[:, :])
        p.dma("G", dbg_sm[:, 32:36], b_bnd[:, 0:8].bitcast(F32))
        tok = p.dma("G", dbg_ekv[:, 0:2048], s_ekv[0].bitcast(F32))
        st["out_last"] = max(st.get("out_last", 0), tok)

    for s_, sems in (
        ("Y", (p.Y, p.ct, p.cw, p.winS0, p.winS1, p.aw, p.xrl, p.bflA,
               p.bflB, p.xlA, p.xlB, p.xs, p.stA, p.stB)),
        ("G", (p.G, p.wg, p.cr, p.sp0, p.sp1, p.sp2, p.sp3, p.cc2,
               p.oA, p.oB)),
        ("S", (p.SQ, p.wq0, p.wq1)),
    ):
        for sem in sems:
            p.wait(s_, sem, sem.n)

    _blk = nc.Block()
    block = _blk.__enter__()
    for name, attr in (("Y", "sync"), ("G", "gpsimd"), ("T", "tensor"),
                       ("V", "vector"), ("S", "scalar")):
        def run_ops(eng, ops=p.st[name]):
            for f in ops:
                f(eng)
        getattr(block, attr)(run_ops)
    _blk.__exit__(None, None, None)
    return nc


def _stage_inputs(inputs):
    x = np.asarray(inputs["L0_attn_x"], np.float32)
    win_t = np.ascontiguousarray(np.asarray(inputs["L0_ffn_w_in"],
                                            np.float32).T)
    w1_t = np.ascontiguousarray(np.asarray(inputs["L0_ffn_w1"], np.float32).T)
    wout_t = np.ascontiguousarray(np.asarray(inputs["L0_ffn_w_out"],
                                             np.float32).T)
    w2_t = np.ascontiguousarray(
        np.asarray(inputs["L0_ffn_w2"], np.float32).transpose(0, 2, 1))
    wge_t = np.ascontiguousarray(np.concatenate(
        [np.asarray(inputs["L0_ffn_wg"], np.float32).T,
         np.asarray(inputs["L0_ffn_we"], np.float32).T], axis=1))
    gbeb = np.ascontiguousarray(np.broadcast_to(np.concatenate(
        [np.asarray(inputs["L0_ffn_gb"], np.float32),
         np.asarray(inputs["L0_ffn_eb"], np.float32)]), (128, 10)))
    wk_t = np.ascontiguousarray(np.asarray(inputs["L0_attn_Wk"], np.float32).T)
    wv_t = np.ascontiguousarray(np.asarray(inputs["L0_attn_Wv"], np.float32).T)
    wr_t = np.ascontiguousarray(np.asarray(inputs["L0_attn_Wr"], np.float32).T)
    wo_t = np.ascontiguousarray(np.asarray(inputs["L0_attn_Wo"], np.float32).T)
    tvec = np.ascontiguousarray(np.stack([
        np.asarray(inputs["L0_attn_time_decay"], np.float32),
        np.asarray(inputs["L0_attn_time_first"], np.float32),
        np.asarray(inputs["L0_attn_time_mix_k"], np.float32),
        np.asarray(inputs["L0_attn_time_mix_v"], np.float32),
        np.asarray(inputs["L0_attn_time_mix_r"], np.float32),
    ]).reshape(5, DC, 128, 1))
    ident = np.ascontiguousarray(np.eye(128, dtype=bfloat16))

    in_maps = []
    for c in range(8):
        b, h = c // 2, c % 2
        if h == 0:
            prev = np.zeros((128, D), np.float32)
            own = x[b, 0:TOWN]
        else:
            prev = x[b, TOWN - 128:TOWN]
            own = x[b, TOWN:2 * TOWN]
        xin = np.ascontiguousarray(np.concatenate([prev, own], axis=0))
        masks = np.zeros((128, 2), np.float32)
        masks[:, 0] = 1.0 if h == 0 else 0.0
        masks[:, 1] = 1.0 if h == 1 else 0.0
        in_maps.append({
            "xin": xin, "win_t": win_t, "w1_t": w1_t, "wout_t": wout_t,
            "w2_t": w2_t, "wge_t": wge_t, "gbeb": gbeb,
            "wk_t": wk_t, "wv_t": wv_t, "wr_t": wr_t, "wo_t": wo_t,
            "tvec": tvec, "masks": masks, "ident": np.asarray(ident),
        })
    return in_maps


def kernel(**inputs):
    if "nc" not in _CACHE:
        _CACHE["nc"] = build_program()
    nc = _CACHE["nc"]
    in_maps = _stage_inputs(inputs)
    res = run_bass_kernel_spmd(nc, in_maps, core_ids=list(range(8)))
    out = np.empty((4, 2 * TOWN, D), np.float32)
    for c in range(8):
        b, h = c // 2, c % 2
        out[b, h * TOWN:(h + 1) * TOWN] = res.results[c]["y"]
    return out


# version 03
# speedup vs baseline: 1.0919x; 1.0919x over previous
import numpy as np
import ml_dtypes
import concourse.bass as bass
import concourse.mybir as mybir
from concourse.bass_utils import run_bass_kernel_spmd

F32 = mybir.dt.float32
BF16 = mybir.dt.bfloat16
AF = mybir.ActivationFunctionType
ALU = mybir.AluOpType
AX = mybir.AxisListType
bfloat16 = ml_dtypes.bfloat16

D = 1024
DC = 8
HID = 2048
NJC = 16
NE = 8
TOWN = 4096
NG = 4
GT = 1024
GW0 = GT + 128

_CACHE = {}
SERIAL_DMA = False


class Sem:
    def __init__(self, nc, name):
        self.h = nc.alloc_semaphore(name)
        self.n = 0


class Prog:
    def __init__(self, nc):
        self.nc = nc
        self.phase = "init"
        self.st = {"T": [], "V": [], "S": [], "Y": [], "G": []}
        self.T = Sem(nc, "sT")
        self.V = Sem(nc, "sV")
        self.S = Sem(nc, "sS")
        self.Y = Sem(nc, "sY")
        self.G = Sem(nc, "sG")
        self.SQ = Sem(nc, "sSQ")
        self.CC = Sem(nc, "sCC")
        for nm in ("ct", "cw", "cr", "wg", "xlA", "xlB", "winS0", "winS1",
                   "aw", "xrl", "bflA", "bflB", "xs", "sp0", "sp1", "sp2",
                   "sp3", "cc2", "oA", "oB", "wq0", "wq1", "stA", "stB"):
            setattr(self, nm, Sem(nc, "s_" + nm))

    def _app(self, s, fn):
        self.st[s].append((self.phase, fn))

    def wait(self, s, sem, tok):
        if tok and tok > 0:
            self._app(s, lambda e, h=sem.h, t=tok: e.wait_ge(h, t))

    def waitt(self, s, t):
        if t:
            sem, v = t
            self.wait(s, sem, v)

    def op(self, s, fn, inc=False):
        if s in ("V", "S"):
            sem = getattr(self, s)
            self.wait(s, sem, sem.n)
            sem.n += 1
            self._app(s, lambda e, f=fn, h=sem.h: f(e).then_inc(h, 1))
            return sem.n
        if inc:
            sem = getattr(self, s)
            sem.n += 1
            self._app(s, lambda e, f=fn, h=sem.h: f(e).then_inc(h, 1))
            return sem.n
        self._app(s, fn)
        return None

    def dma(self, s, dst, src, rs=None):
        sem = rs if rs is not None else {"Y": self.Y, "G": self.G,
                                         "S": self.SQ}[s]
        sem.n += 16
        self._app(
            s, lambda e, d=dst, i=src, h=sem.h:
            e.dma_start(out=d, in_=i).then_inc(h, 16))
        if SERIAL_DMA:
            tok = sem.n
            self._app(s, lambda e, h=sem.h, t=tok: e.wait_ge(h, t))
        return sem.n


def build_program(stage=5, debug=False):
    nc = bass.Bass("TRN2", target_bir_lowering=False, debug=False)
    p = Prog(nc)

    def din(name, shape, dt=F32):
        return nc.dram_tensor(name, list(shape), dt, kind="ExternalInput").ap()

    d_xin = din("xin", [128 + TOWN, D])
    d_win = din("win_t", [D, 2 * HID])
    d_w1 = din("w1_t", [D, 2 * HID])
    d_wout = din("wout_t", [HID, D])
    d_w2 = din("w2_t", [NE, HID, D])
    d_wge = din("wge_t", [D, 10])
    d_gbeb = din("gbeb", [128, 10])
    d_wk = din("wk_t", [D, D])
    d_wv = din("wv_t", [D, D])
    d_wr = din("wr_t", [D, D])
    d_wo = din("wo_t", [D, D])
    d_tvec = din("tvec", [5, DC, 128, 1])
    d_mask = din("masks", [128, 2])
    d_ident = din("ident", [128, 128], BF16)
    d_y = nc.dram_tensor("y", [TOWN, D], F32, kind="ExternalOutput").ap()
    if debug:
        dbg_x1 = nc.dram_tensor("dbg_x1", [TOWN, D], F32,
                                kind="ExternalOutput").ap()
        dbg_x2 = nc.dram_tensor("dbg_x2", [TOWN, D], F32,
                                kind="ExternalOutput").ap()
        dbg_sm = nc.dram_tensor("dbg_sm", [128, 48], F32,
                                kind="ExternalOutput").ap()
        dbg_ekv = nc.dram_tensor("dbg_ekv", [128, TOWN], F32,
                                 kind="ExternalOutput").ap()
        dbg_xlnT = nc.dram_tensor("dbg_xlnT", [128, GW0], BF16,
                                  kind="ExternalOutput").ap()
        dbg_h = nc.dram_tensor("dbg_h", [128, GW0], BF16,
                               kind="ExternalOutput").ap()
        dbg_acc = nc.dram_tensor("dbg_acc", [128, D], F32,
                                 kind="ExternalOutput").ap()
        dbg_wse = nc.dram_tensor("dbg_wse", [128, 72], F32,
                                 kind="ExternalOutput").ap()
        dbg_w = nc.dram_tensor("dbg_w", [128, 4096], BF16,
                               kind="ExternalOutput").ap()
        dbg_rt = nc.dram_tensor("dbg_rt", [128, 9 * 64], F32,
                                kind="ExternalOutput").ap()

    def scr(name, shape, dt):
        return nc.dram_tensor(name, list(shape), dt, kind="Internal").ap()

    s_wb = {
        "win": scr("winb", [D, 2 * HID], BF16),
        "w1": scr("w1b", [D, 2 * HID], BF16),
        "wout": scr("woutb", [HID, D], BF16),
        "wk": scr("wkb", [D, D], BF16),
        "wv": scr("wvb", [D, D], BF16),
        "wr": scr("wrb", [D, D], BF16),
        "wo": scr("wob", [D, D], BF16),
    }
    for e_ in range(NE):
        s_wb[f"w2_{e_}"] = scr(f"w2b{e_}", [HID, D], BF16)
    s_x1 = scr("x1s", [TOWN, D], F32)
    s_x2 = scr("x2s", [TOWN, D], F32)
    s_xr = scr("xrs", [DC, 128, TOWN], BF16)
    s_ekv = scr("ekvs", [DC, 128, TOWN], BF16)
    s_ek = scr("eks", [DC, 128, TOWN], BF16)
    cc_in = nc.dram_tensor("cc_in", [128, 16], F32, kind="Internal")
    cc_out = nc.dram_tensor("cc_out", [128, 16], F32, kind="Internal")

    def sb(name, shape, dt):
        return nc.alloc_sbuf_tensor(name, list(shape), dt)

    b_x = sb("b_x", [128, 2 * D], F32)
    b_xln = sb("b_xln", [128, 2 * D], BF16)
    b_xlnT = sb("b_xlnT", [128, DC * GW0], BF16)
    b_h = sb("b_h", [128, NJC * GW0], BF16)
    b_acc = sb("b_acc", [128, 9 * D], F32)
    b_accb = b_acc.ap().bitcast(BF16)
    b_win = sb("b_win", [128, 2 * DC * 256], BF16)
    b_winf = b_win.ap().bitcast(F32)
    b_W = sb("b_W", [128, 2 * 16 * D], BF16)
    b_F = sb("b_F", [128, 5 * 1032], F32)
    b_bfl = sb("b_bfl", [128, 4 * D], BF16)
    b_conv = sb("b_conv", [128, 1024], BF16)
    b_wge = sb("b_wge", [128, DC * 10], BF16)
    b_gbeb = sb("b_gbeb", [128, 10], F32)
    b_tv = sb("b_tv", [128, 5 * DC], F32)
    b_drv = sb("b_drv", [128, 3 * DC], F32)
    b_mask = sb("b_mask", [128, 2], F32)
    b_id = sb("b_id", [128, 128], BF16)
    b_ends = sb("b_ends", [128, 16], F32)
    b_cin = sb("b_cin", [128, 16], F32)
    b_stage = sb("b_stage", [128, 16], F32)
    b_bnd = sb("b_bnd", [128, DC], BF16)
    b_wse = sb("b_wse", [128, 9 * 8], F32)
    b_rt = sb("b_rt", [128, 9 * 64], F32)
    b_st = sb("b_st", [128, 24], F32)

    PS0T = nc.alloc_psum_tensor("PS0", [128, 1024], BF16)
    PS = [PS0T] + [nc.alloc_psum_tensor(f"PS{i}", [128, 512], F32)
                   for i in range(1, 8)]
    PS0b = PS0T.ap()

    def F(i, w=1024, off=0):
        return b_F[:, i * 1032 + off: i * 1032 + off + w]

    def Wtile(slot, i, c0=0, cw=D):
        base = slot * 16 * D + i * D
        return b_W[:, base + c0: base + c0 + cw]

    def hs(j, w, off=0):
        return b_h[:, j * GW0 + off: j * GW0 + off + w]

    def xlnT(dc, off, w):
        return b_xlnT[:, dc * GW0 + off: dc * GW0 + off + w]

    def tvcol(v, dc):
        return b_tv[:, v * DC + dc: v * DC + dc + 1]

    def drv(v, dc):
        return b_drv[:, v * DC + dc: v * DC + dc + 1]

    st = {}

    def ps_wait(idxs):
        for i in (idxs if isinstance(idxs, (list, tuple)) else [idxs]):
            p.wait("T", p.V, st.get(f"psv{i}", 0))
            p.wait("T", p.S, st.get(f"pss{i}", 0))

    def ps_used(i, eng, tok):
        st[f"ps{'v' if eng == 'V' else 's'}{i}"] = tok

    PAIR = {0: (2, 3), 1: (5, 6)}
    TRIPLE = {0: (2, 3, 4), 1: (5, 6, 7)}

    p.phase = "pro"
    t_cst = 0
    for dst, src in [(b_gbeb[:, :], d_gbeb), (b_mask[:, :], d_mask),
                     (b_id[:, :], d_ident)]:
        t_cst = p.dma("Y", dst, src, rs=p.ct)
    for v in range(5):
        for dc in range(DC):
            t_cst = p.dma("Y", tvcol(v, dc), d_tvec[v, dc], rs=p.ct)
    t_wge = 0
    for dc in range(DC):
        t_wge = p.dma("G", b_wge[:, dc * 10:(dc + 1) * 10],
                      d_wge[dc * 128:(dc + 1) * 128, :], rs=p.wg)
    st["wge"] = t_wge

    p.wait("S", p.ct, t_cst)
    t_drv = 0
    for dc in range(DC):
        p.op("S", lambda e, o=drv(0, dc), i=tvcol(0, dc):
             e.activation(o, i, AF.Exp))
        p.op("S", lambda e, o=drv(0, dc), i=drv(0, dc): e.mul(o, i, -1.0))
        p.op("S", lambda e, o=drv(1, dc), i=drv(0, dc):
             e.activation(o, i, AF.Exp))
        t_drv = p.op("S", lambda e, o=drv(2, dc), i=tvcol(1, dc):
                     e.activation(o, i, AF.Exp), inc=True)
    st["drv"] = t_drv
    p.wait("V", p.S, t_drv)
    p.op("V", lambda e: e.memset(b_ends[:, :], 0.0))
    st["bnd0"] = p.op("V", lambda e: e.memset(b_bnd[:, :], 0.0), inc=True)

    conv_done = {}
    conv_state = {"war": 0}
    conv_pieces = {
        "win": (d_win, s_wb["win"], D, 2 * HID),
        "w1": (d_w1, s_wb["w1"], D, 2 * HID),
        "wout": (d_wout, s_wb["wout"], HID, D),
        "wk": (d_wk, s_wb["wk"], D, D),
        "wv": (d_wv, s_wb["wv"], D, D),
        "wr": (d_wr, s_wb["wr"], D, D),
        "wo": (d_wo, s_wb["wo"], D, D),
    }
    for e_ in range(NE):
        conv_pieces[f"w2_{e_}"] = (d_w2[e_], s_wb[f"w2_{e_}"], HID, D)

    conv_slots = [b_conv[:, 0:1024]] + [
        b_bfl[:, i * 1024:(i + 1) * 1024] for i in range(4)]
    conv_war = [0] * 5

    def conv(name):
        if name in conv_done:
            return conv_done[name]
        src, dst, rows, cols = conv_pieces[name]
        tok_y = 0
        sl = conv_state.get("sl", 0)
        for r0 in range(0, rows, 128):
            for c0 in range(0, cols, 1024):
                w = min(1024, cols - c0)
                cb = conv_slots[sl][:, 0:w]
                p.wait("G", p.cw, conv_war[sl])
                tg = p.dma("G", cb, src[r0:r0 + 128, c0:c0 + w], rs=p.cr)
                p.wait("Y", p.cr, tg)
                tok_y = p.dma("Y", dst[r0:r0 + 128, c0:c0 + w], cb, rs=p.cw)
                conv_war[sl] = tok_y
                sl = (sl + 1) % 5
        conv_state["sl"] = sl
        conv_done[name] = tok_y
        return tok_y

    def ln_transpose(sub, xap, parts="full"):
        xlnap = b_xln[:, (sub % 2) * D:(sub % 2 + 1) * D]
        p.wait("V", p.T, st.get(f"xlnfree{sub % 2}", 0))
        par_ = sub % 2
        stt = b_st[:, par_ * 8:par_ * 8 + 8]
        p.wait("V", p.S, st.get(f"lnstatS{par_}", 0))
        p.op("V", lambda e, o=stt[:, 0:1], i=xap: e.reduce_sum(o, i, axis=AX.X))
        p.wait("S", p.T, st.get("winfree0", 0))
        tok_s = p.op("S", lambda e, o=b_winf[:, 0:D], i=xap,
                     a=stt[:, 1:2]:
                     e.activation(o, i, AF.Square, accum_out=a), inc=True)
        st["sqtok"] = tok_s
        tok_m = p.op("V", lambda e, o=stt[:, 2:3], i=stt[:, 0:1]:
                     e.tensor_scalar_mul(o, i, 1.0 / D), inc=True)
        p.wait("S", p.V, tok_m)
        tok_s2 = p.op("S", lambda e, o=stt[:, 3:4], i=stt[:, 2:3]:
                      e.activation(o, i, AF.Square), inc=True)
        p.wait("V", p.S, tok_s)
        p.op("V", lambda e, o=stt[:, 4:5], i=stt[:, 1:2]:
             e.tensor_scalar_mul(o, i, 1.0 / D))
        p.wait("V", p.S, tok_s2)
        p.op("V", lambda e, o=stt[:, 4:5], a=stt[:, 4:5], b=stt[:, 3:4]:
             e.tensor_sub(o, a, b))
        tok_v = p.op("V", lambda e, o=stt[:, 4:5], i=stt[:, 4:5]:
                     e.tensor_scalar_add(o, i, 1e-5), inc=True)
        p.wait("S", p.V, tok_v)
        tok_s = p.op("S", lambda e, o=stt[:, 5:6], i=stt[:, 4:5]:
                     e.activation(o, i, AF.Sqrt), inc=True)
        st[f"lnstatS{par_}"] = tok_s
        p.wait("V", p.S, tok_s)
        p.op("V", lambda e, o=stt[:, 6:7], i=stt[:, 5:6]: e.reciprocal(o, i))
        ln_tok = p.op("V", lambda e, o=xlnap, i=xap, m=stt[:, 2:3],
                      r=stt[:, 6:7]:
                      e.tensor_scalar(o, i, m, r, ALU.subtract, ALU.mult),
                      inc=True)
        if stage < 0.18 or parts == "ln":
            return (ln_tok, 1)
        p.wait("T", p.V, ln_tok)
        ps_wait(0)
        ttok = None
        for dc in range(DC):
            ttok = p.op("T", lambda e, o=PS0b[:, dc * 128:(dc + 1) * 128],
                        i=xlnap[:, dc * 128:(dc + 1) * 128], idn=b_id[:, :]:
                        e.transpose(o, i, idn), inc=(dc == DC - 1))
        st[f"xlnfree{sub % 2}"] = ttok
        if stage < 0.22 or parts == "tr":
            return (ttok, 1)
        vt = stk = None
        for dc in range(DC):
            s = "V"
            p.wait(s, p.T, ttok)
            if s == "V":
                tok = p.op(s, lambda e, o=xlnT(dc, sub * 128, 128),
                           i=PS0b[:, dc * 128:(dc + 1) * 128]:
                           e.tensor_copy(o, i), inc=True)
            else:
                tok = p.op(s, lambda e, o=xlnT(dc, sub * 128, 128),
                           i=PS0b[:, dc * 128:(dc + 1) * 128]:
                           e.copy(o, i), inc=True)
            if s == "V":
                vt = tok
            else:
                stk = tok
        ps_used(0, "V", vt)
        ps_used(0, "S", stk)
        return vt, stk

    def ffn_core(pass_id, g, x_resid_src, store_dst, tr_toks,
                 store_q="Y", epi=None):
        if stage < 0.5:
            return []
        nsub = 9 if (g == 0 and pass_id == "p1") else 8
        w = nsub * 128
        tsp = ([(0, 128), (128, 512), (640, 512)] if w == GW0
               else [(0, 512), (512, 512)])
        first = pass_id == "p1" and g == 0

        p.phase = f"{pass_id}g{g}_gate"
        for sub in range(nsub):
            vt, stk = tr_toks[sub]
            p.wait("T", p.V, vt)
            p.wait("T", p.S, stk)
            if sub == 0:
                p.wait("T", p.wg, st["wge"])
            p.wait("T", p.V, st.get("gatefree", 0))
            ps = PS[1][:, 0:10]
            gtok = None
            for dc in range(DC):
                gtok = p.op("T", lambda e, o=ps,
                            l=xlnT(dc, sub * 128, 128),
                            r=b_wge[:, dc * 10:(dc + 1) * 10],
                            s0=(dc == 0), s1=(dc == DC - 1):
                            e.matmul(o, l, r, start=s0, stop=s1),
                            inc=(dc == DC - 1))
            p.wait("V", p.T, gtok)
            tok = p.op("V", lambda e, o=b_rt[:, sub * 64: sub * 64 + 10],
                       a=ps, b=b_gbeb[:, :]: e.tensor_add(o, a, b), inc=True)
            st["gatefree"] = tok
        ps_used(1, "V", st["gatefree"])

        def rtc(sub, c0, c1=None):
            c1 = c0 + 1 if c1 is None else c1
            return b_rt[:, sub * 64 + c0: sub * 64 + c1]

        vtok = None
        for sub in range(nsub):
            p.op("V", lambda e, o=rtc(sub, 10), a=rtc(sub, 0), b=rtc(sub, 1):
                 e.tensor_sub(o, a, b))
            p.op("V", lambda e, o=rtc(sub, 11), i=rtc(sub, 10):
                 e.tensor_single_scalar(o, i, 0.0, ALU.is_ge))
            p.op("V", lambda e, o=rtc(sub, 12), i=rtc(sub, 2, 6):
                 e.reduce_max(o, i, axis=AX.X))
            p.op("V", lambda e, o=rtc(sub, 13), i=rtc(sub, 6, 10):
                 e.reduce_max(o, i, axis=AX.X))
            p.op("V", lambda e, o=rtc(sub, 12), i=rtc(sub, 12):
                 e.tensor_scalar_mul(o, i, -1.0))
            vtok = p.op("V", lambda e, o=rtc(sub, 13), i=rtc(sub, 13):
                        e.tensor_scalar_mul(o, i, -1.0), inc=True)
        p.wait("S", p.V, vtok)
        stok = None
        for sub in range(nsub):
            p.op("S", lambda e, o=rtc(sub, 14), i=rtc(sub, 10):
                 e.activation(o, i, AF.Abs))
            p.op("S", lambda e, o=rtc(sub, 15), i=rtc(sub, 14):
                 e.activation(o, i, AF.Sigmoid))
        for sub in range(nsub):
            p.op("S", lambda e, o=rtc(sub, 18, 22), i=rtc(sub, 2, 6),
                 b=rtc(sub, 12), a=rtc(sub, 26):
                 e.activation(o, i, AF.Exp, bias=b, accum_out=a))
            stok = p.op("S", lambda e, o=rtc(sub, 22, 26), i=rtc(sub, 6, 10),
                        b=rtc(sub, 13), a=rtc(sub, 27):
                        e.activation(o, i, AF.Exp, bias=b, accum_out=a),
                        inc=True)
        p.wait("V", p.S, stok)
        wse_tok = None
        for sub in range(nsub):
            p.op("V", lambda e, o=rtc(sub, 28), i=rtc(sub, 26):
                 e.reciprocal(o, i))
            p.op("V", lambda e, o=rtc(sub, 29), i=rtc(sub, 27):
                 e.reciprocal(o, i))
            p.op("V", lambda e, o=rtc(sub, 16), a=rtc(sub, 15), b=rtc(sub, 11):
                 e.tensor_mul(o, a, b))
            p.op("V", lambda e, o=rtc(sub, 17), a=rtc(sub, 15), b=rtc(sub, 16):
                 e.tensor_sub(o, a, b))
            p.op("V", lambda e, o=rtc(sub, 30), a=rtc(sub, 16), b=rtc(sub, 28):
                 e.tensor_mul(o, a, b))
            p.op("V", lambda e, o=rtc(sub, 31), a=rtc(sub, 17), b=rtc(sub, 29):
                 e.tensor_mul(o, a, b))
            p.op("V", lambda e, o=rtc(sub, 32, 36), i=rtc(sub, 18, 22),
                 s=rtc(sub, 30): e.tensor_scalar_mul(o, i, s))
            p.op("V", lambda e, o=rtc(sub, 36, 40), i=rtc(sub, 22, 26),
                 s=rtc(sub, 31): e.tensor_scalar_mul(o, i, s))
            p.op("V", lambda e, o=rtc(sub, 40), i=rtc(sub, 32, 40):
                 e.reduce_max(o, i, axis=AX.X))
            p.op("V", lambda e, o=rtc(sub, 41, 49), i=rtc(sub, 32, 40),
                 s=rtc(sub, 40): e.tensor_single_scalar(o, i, s, ALU.is_ge))
            p.op("V", lambda e, o=rtc(sub, 49, 57), i=rtc(sub, 41, 49),
                 b=rtc(sub, 32, 40):
                 e.scalar_tensor_tensor(o, i, -2.0, b, ALU.mult, ALU.add))
            p.op("V", lambda e, o=rtc(sub, 57), i=rtc(sub, 49, 57):
                 e.reduce_max(o, i, axis=AX.X))
            p.op("V", lambda e, o=rtc(sub, 41, 49), i=rtc(sub, 32, 40),
                 s=rtc(sub, 57): e.tensor_single_scalar(o, i, s, ALU.is_ge))
            p.op("V", lambda e, o=rtc(sub, 49, 57), a=rtc(sub, 41, 49),
                 b=rtc(sub, 32, 40): e.tensor_mul(o, a, b))
            wse_tok = p.op("V", lambda e, o=b_wse[:, sub * 8:(sub + 1) * 8],
                           i=rtc(sub, 49, 57):
                           e.tensor_scalar_mul(o, i, 0.5), inc=True)

        if stage < 0.6:
            return []
        jblk_tok = [None] * 32

        def emit_jblk(k):
            if k >= 32 or jblk_tok[k] is not None:
                return
            wname = "win" if k < 16 else "w1"
            jb = k % 16
            slot = k % 2
            base = slot * DC * 256
            if first:
                conv(wname)
                p.wait("Y", p.cw, conv_done[wname])
            p.wait("Y", p.T, st.get(f"winfree{slot}", 0))
            if slot == 0:
                p.wait("Y", p.S, st.get("sqtok", 0))
            ws = p.winS0 if slot == 0 else p.winS1
            tok = None
            for dc in range(DC):
                tok = p.dma(
                    "Y", b_win[:, base + dc * 256: base + (dc + 1) * 256],
                    s_wb[wname][dc * 128:(dc + 1) * 128,
                                jb * 256:(jb + 1) * 256], rs=ws)
            jblk_tok[k] = (ws, tok)

        emit_jblk(0)
        emit_jblk(1)

        def load_W(slot, wname, war_tok):
            if first:
                cv = conv(wname)
                p.wait("S", p.cw, cv)
            p.wait("S", p.T, war_tok)
            wq = p.wq0 if slot == 0 else p.wq1
            tok = None
            for i in range(16):
                tok = p.dma("S", Wtile(slot, i),
                            s_wb[wname][i * 128:(i + 1) * 128, :], rs=wq)
            return (wq, tok)

        wout_tok = load_W(0, "wout", st.get("Wfree0", 0))
        w2_tok = {0: load_W(1, "w2_0", st.get("Wfree1", 0))}

        def h_prod(m_i):
            silu_toks = [None] * NJC
            last_v = None
            for jc in range(2 * NJC):
                k = m_i * 16 + jc // 2
                slot = k % 2
                base = slot * DC * 256 + (jc % 2) * 128
                p.waitt("T", jblk_tok[k])
                if jc == 0:
                    p.wait("T", p.xs, st.get("hfreeY", 0))
                is_a = jc < NJC
                hj = jc if is_a else jc - NJC
                par = jc % 2
                tiles = TRIPLE[par][:len(tsp)]
                ps_wait(list(tiles))
                mms = []
                for ci, (t0, tn) in enumerate(tsp):
                    mm = None
                    for dc in range(DC):
                        mm = p.op(
                            "T",
                            lambda e, o=PS[tiles[ci]][:, 0:tn],
                            l=b_win[:, base + dc * 256: base + dc * 256 + 128],
                            r=xlnT(dc, t0, tn),
                            s0=(dc == 0), s1=(dc == DC - 1):
                            e.matmul(o, l, r, start=s0, stop=s1),
                            inc=(dc == DC - 1))
                    mms.append(mm)
                if jc % 2 == 1:
                    st[f"winfree{slot}"] = mms[-1]
                    emit_jblk(k + 2)
                if is_a:
                    tok = None
                    for ci, (t0, tn) in enumerate(tsp):
                        p.wait("S", p.T, mms[ci])
                        tok = p.op("S", lambda e, o=hs(hj, tn, t0),
                                   i=PS[tiles[ci]][:, 0:tn]:
                                   e.activation(o, i, AF.Silu), inc=True)
                        ps_used(tiles[ci], "S", tok)
                    silu_toks[hj] = tok
                else:
                    p.wait("V", p.S, silu_toks[hj])
                    for ci, (t0, tn) in enumerate(tsp):
                        p.wait("V", p.T, mms[ci])
                        tok = p.op("V", lambda e, o=hs(hj, tn, t0),
                                   a=hs(hj, tn, t0), b=PS[tiles[ci]][:, 0:tn]:
                                   e.tensor_mul(o, a, b), inc=True)
                        ps_used(tiles[ci], "V", tok)
                    last_v = tok
            return last_v

        if pass_id == "p4":
            p.wait("S", p.V, st.get("fin", 0))
        if stage < 0.7:
            return []
        p.phase = f"{pass_id}g{g}_h1"
        h1_v = h_prod(0)
        if debug and pass_id == "p1" and g == 0:
            p.wait("G", p.V, h1_v)
            p.dma("G", dbg_xlnT[:, :], b_xlnT[:, 0:GW0])
            p.dma("G", dbg_h[:, :], b_h[:, 0:GW0])
            p.dma("G", dbg_wse[:, :], b_wse[:, :])
            p.dma("G", dbg_rt[:, :], b_rt[:, :])
            dtok = p.dma("G", dbg_w[:, :], s_wb["win"][0:128, :])
            p.wait("V", p.G, dtok)
            p.wait("S", p.G, dtok)
            p.wait("T", p.G, dtok)
        if stage < 0.8:
            return []

        p.phase = f"{pass_id}g{g}_sh"
        p.waitt("T", wout_tok)
        p.wait("T", p.V, h1_v)
        if pass_id == "p4":
            p.wait("V", p.T, st.get("xr_T_done", 0))
        for sub in range(nsub):
            par = sub % 2
            pr = PAIR[par]
            ps_wait(list(pr))
            mms = [None, None]
            for jc in range(NJC):
                for eh in range(2):
                    mms[eh] = p.op(
                        "T",
                        lambda e, o=PS[pr[eh]][:, :],
                        l=hs(jc, 128, sub * 128),
                        r=Wtile(0, jc, eh * 512, 512),
                        s0=(jc == 0), s1=(jc == NJC - 1):
                        e.matmul(o, l, r, start=s0, stop=s1),
                        inc=(jc == NJC - 1))
            for eh in range(2):
                p.wait("V", p.T, mms[eh])
                tok = p.op("V", lambda e,
                           o=b_acc[:, sub * D + eh * 512:
                                   sub * D + eh * 512 + 512],
                           i=PS[pr[eh]][:, :]:
                           e.tensor_scalar_mul(o, i, 0.5), inc=True)
                ps_used(pr[eh], "V", tok)
        wout_T_done = p.T.n
        if debug and pass_id == "p1" and g == 0:
            p.wait("G", p.V, p.V.n)
            dtok = p.dma("G", dbg_acc[:, :], b_acc[:, 0:D])
            p.wait("V", p.G, dtok)
            p.wait("T", p.G, dtok)
        if stage < 0.85:
            return []

        p.phase = f"{pass_id}g{g}_hh"
        hh_v = h_prod(1)

        res = []

        def do_res(sub):
            p.phase = f"{pass_id}g{g}_res"
            skip = (pass_id == "p1" and g == 0 and sub == 0)
            xslot = b_x[:, (sub % 2) * D:(sub % 2 + 1) * D]
            p.wait("Y", p.V, st.get(f"xresfree{sub % 2}", 0))
            p.wait("Y", p.S, st.get(f"xresfreeS{sub % 2}", 0))
            p.waitt("Y", st.get(f"xresfreeG{sub % 2}"))
            p.waitt("Y", st.get(f"xstore{sub % 2}"))
            xl = p.xlA if sub % 2 == 0 else p.xlB
            ytok = p.dma("Y", xslot, x_resid_src(sub), rs=xl)
            p.wait("V", xl, ytok)
            tok = p.op("V", lambda e, o=xslot, a=xslot,
                       b=b_acc[:, sub * D:(sub + 1) * D]:
                       e.tensor_add(o, a, b), inc=True)
            res.append((tok, xslot, skip))
            dst = store_dst(sub)
            if dst is not None and not skip:
                p.wait(store_q, p.V, tok)
                if store_q == "G":
                    so = p.oA if sub % 2 == 0 else p.oB
                else:
                    so = p.stA if sub % 2 == 0 else p.stB
                stok = p.dma(store_q, dst, xslot, rs=so)
                if store_q == "G":
                    st[f"xresfreeG{sub % 2}"] = (so, stok)
                else:
                    st[f"xstore{sub % 2}"] = (so, stok)
            if epi is not None:
                epi(sub, tok, xslot)
            else:
                st[f"xresfree{sub % 2}"] = tok

        if stage < 0.9:
            return []
        p.phase = f"{pass_id}g{g}_rt"
        p.wait("T", p.V, hh_v)
        p.wait("T", p.V, wse_tok)
        w2_T_done = {}
        for e_ in range(NE):
            slot = 1 - (e_ % 2)
            p.waitt("T", w2_tok[e_])
            if e_ + 1 < NE:
                war = wout_T_done if e_ == 0 else w2_T_done[e_ - 1]
                w2_tok[e_ + 1] = load_W(1 - slot, f"w2_{e_ + 1}", war)
            for sub in range(nsub):
                par = sub % 2
                pr = PAIR[par]
                ps_wait(list(pr))
                mms = [None, None]
                for jc in range(NJC):
                    for eh in range(2):
                        mms[eh] = p.op(
                            "T",
                            lambda e, o=PS[pr[eh]][:, :],
                            l=hs(jc, 128, sub * 128),
                            r=Wtile(slot, jc, eh * 512, 512),
                            s0=(jc == 0), s1=(jc == NJC - 1):
                            e.matmul(o, l, r, start=s0, stop=s1),
                            inc=(jc == NJC - 1))
                for eh in range(2):
                    p.wait("V", p.T, mms[eh])
                    tok = p.op("V", lambda e,
                               o=b_acc[:, sub * D + eh * 512:
                                       sub * D + eh * 512 + 512],
                               i=PS[pr[eh]][:, :],
                               s=b_wse[:, sub * 8 + e_: sub * 8 + e_ + 1],
                               a=b_acc[:, sub * D + eh * 512:
                                       sub * D + eh * 512 + 512]:
                               e.scalar_tensor_tensor(o, i, s, a, ALU.mult,
                                                      ALU.add), inc=True)
                    ps_used(pr[eh], "V", tok)
                if e_ == NE - 1:
                    do_res(sub)
                    p.phase = f"{pass_id}g{g}_rt"
            w2_T_done[e_] = p.T.n
        st["hfree"] = p.T.n
        st["Wfree0"] = p.T.n
        st["Wfree1"] = p.T.n

        if stage < 0.95:
            return []
        st["accfree"] = res[-1][0]
        return res

    def p1_tail(g, tr2):
        p.phase = f"p1g{g}_tail"
        nsub = 9 if g == 0 else 8
        w = nsub * 128
        own0 = 128 if g == 0 else 0
        ln2s = max((t[1] or 0) for t in tr2)
        if stage < 1.2:
            return

        if g == 0:
            conv("wk")
            cv = conv("wv")
            p.wait("Y", p.cw, cv)
        p.wait("Y", p.T, st["Wfree0"])
        wk_tok = None
        for dc in range(DC):
            wk_tok = p.dma("Y", Wtile(0, dc),
                           s_wb["wk"][dc * 128:(dc + 1) * 128, :], rs=p.aw)
        for dc in range(DC):
            wk_tok = p.dma("Y", Wtile(0, 8 + dc),
                           s_wb["wv"][dc * 128:(dc + 1) * 128, :], rs=p.aw)

        p.wait("V", p.S, ln2s)
        p.wait("V", p.T, st["hfree"])
        p.wait("V", p.xs, st.get("hfreeY", 0))
        mix_last = None
        for mi, tv in enumerate((2, 3)):
            for dc in range(DC):
                M0 = mi * 8 + dc
                p.op("V", lambda e, o=b_st[:, 16:17], i=tvcol(tv, dc):
                     e.tensor_scalar(o, i, -1.0, 1.0, ALU.mult, ALU.add))
                p.op("V", lambda e, o=hs(M0, w), i=xlnT(dc, 0, w),
                     s=tvcol(tv, dc): e.tensor_scalar_mul(o, i, s))
                tok = p.op("V", lambda e, o=hs(M0, w - 1, 1),
                           i=xlnT(dc, 0, w - 1), s=b_st[:, 16:17],
                           a=hs(M0, w - 1, 1):
                           e.scalar_tensor_tensor(o, i, s, a, ALU.mult,
                                                  ALU.add), inc=True)
                if g > 0:
                    tok = p.op("V", lambda e, o=hs(M0, 1, 0),
                               i=b_bnd[:, dc:dc + 1], s=b_st[:, 16:17],
                               a=hs(M0, 1, 0):
                               e.scalar_tensor_tensor(o, i, s, a, ALU.mult,
                                                      ALU.add), inc=True)
                mix_last = tok
        if stage < 1.25:
            return
        p.phase = f"p1g{g}_kv"
        p.wait("T", p.V, mix_last)
        p.wait("T", p.aw, wk_tok)
        for ec in range(DC):
            ps_wait([2, 3])
            mmk = [None, None]
            for th in range(2):
                for dc in range(DC):
                    mmk[th] = p.op(
                        "T",
                        lambda e, o=PS[2 + th][:, :],
                        l=Wtile(0, dc, ec * 128, 128),
                        r=hs(dc, 512, own0 + th * 512),
                        s0=(dc == 0), s1=(dc == DC - 1):
                        e.matmul(o, l, r, start=s0, stop=s1),
                        inc=(dc == DC - 1))
            ps_wait([5, 6])
            mmv = [None, None]
            for th in range(2):
                for dc in range(DC):
                    mmv[th] = p.op(
                        "T",
                        lambda e, o=PS[5 + th][:, :],
                        l=Wtile(0, 8 + dc, ec * 128, 128),
                        r=hs(8 + dc, 512, own0 + th * 512),
                        s0=(dc == 0), s1=(dc == DC - 1):
                        e.matmul(o, l, r, start=s0, stop=s1),
                        inc=(dc == DC - 1))
            ekslot, vslot = ec % 2, 2 + ec % 2
            dump = b_acc[:, 8 * D:9 * D]
            p.waitt("S", st.get(f"spillek{ekslot}"))
            ektok = None
            for th in range(2):
                p.wait("S", p.T, mmk[th])
                ektok = p.op("S", lambda e, o=F(ekslot, 512, th * 512),
                             i=PS[2 + th][:, :]:
                             e.activation(o, i, AF.Exp), inc=True)
                ps_used(2 + th, "S", ektok)
            p.waitt("V", st.get(f"spillekv{vslot}"))
            vtok = None
            for th in range(2):
                p.wait("V", p.T, mmv[th])
                vtok = p.op("V", lambda e, o=F(vslot, 512, th * 512),
                            i=PS[5 + th][:, :]: e.tensor_copy(o, i), inc=True)
                ps_used(5 + th, "V", vtok)
            p.wait("V", p.S, ektok)
            ekvtok = p.op("V", lambda e, o=F(vslot), a=F(ekslot), b=F(vslot):
                          e.tensor_mul(o, a, b), inc=True)
            if stage < 1.35:
                continue
            p.op("V", lambda e, o=dump,
                 d0=drv(1, ec).broadcast_to((128, GT)), d1=F(vslot),
                 i0=b_ends[:, ec:ec + 1]:
                 e.tensor_tensor_scan(o, d0, d1, i0, ALU.mult, ALU.add))
            p.op("V", lambda e, o=b_ends[:, ec:ec + 1],
                 i=dump[:, GT - 1:GT]: e.tensor_copy(o, i))
            sbk = p.op("V", lambda e, o=dump,
                       d0=drv(1, ec).broadcast_to((128, GT)), d1=F(ekslot),
                       i0=b_ends[:, 8 + ec:8 + ec + 1]:
                       e.tensor_tensor_scan(o, d0, d1, i0, ALU.mult, ALU.add),
                       inc=True)
            endtok = p.op("V", lambda e, o=b_ends[:, 8 + ec:8 + ec + 1],
                          i=dump[:, GT - 1:GT]: e.tensor_copy(o, i),
                          inc=True)
            st["endtok"] = endtok
            if stage < 1.45:
                continue
            p.wait("G", p.V, ekvtok)
            spv = getattr(p, f"sp{vslot}")
            st[f"spillekv{vslot}"] = (spv, p.dma(
                "G", s_ekv[ec][:, g * GT:(g + 1) * GT], F(vslot), rs=spv))
            p.wait("G", p.V, sbk)
            spk = getattr(p, f"sp{ekslot}")
            st[f"spillek{ekslot}"] = (spk, p.dma(
                "G", s_ek[ec][:, g * GT:(g + 1) * GT], F(ekslot), rs=spk))
        proj_T = p.T.n
        if stage < 1.49:
            return
        p.wait("V", p.T, proj_T)
        mixr_last = None
        for dc in range(DC):
            p.op("V", lambda e, o=b_st[:, 16:17], i=tvcol(4, dc):
                 e.tensor_scalar(o, i, -1.0, 1.0, ALU.mult, ALU.add))
            p.op("V", lambda e, o=hs(dc, w), i=xlnT(dc, 0, w),
                 s=tvcol(4, dc): e.tensor_scalar_mul(o, i, s))
            tok = p.op("V", lambda e, o=hs(dc, w - 1, 1),
                       i=xlnT(dc, 0, w - 1), s=b_st[:, 16:17],
                       a=hs(dc, w - 1, 1):
                       e.scalar_tensor_tensor(o, i, s, a, ALU.mult,
                                              ALU.add), inc=True)
            if g > 0:
                tok = p.op("V", lambda e, o=hs(dc, 1, 0),
                           i=b_bnd[:, dc:dc + 1], s=b_st[:, 16:17],
                           a=hs(dc, 1, 0):
                           e.scalar_tensor_tensor(o, i, s, a, ALU.mult,
                                                  ALU.add), inc=True)
            mixr_last = tok
        if stage < 1.493:
            return
        for dc in range(DC):
            p.op("V", lambda e, o=b_bnd[:, dc:dc + 1],
                 i=xlnT(dc, w - 1, 1): e.tensor_copy(o, i))
        if stage < 1.497:
            return
        hfY = None
        for dc in range(DC):
            p.wait("Y", p.V, mixr_last if dc == 0 else 0)
            hfY = p.dma("Y", s_xr[dc][:, g * GT:(g + 1) * GT],
                        hs(dc, GT, own0), rs=p.xs)
        st["hfreeY"] = hfY
        st["hfree"] = p.T.n
        st["Wfree0"] = p.T.n

    def p2c_group(g):
        p.phase = f"p2cg{g}"
        nsub = 8
        if g == 0:
            for sem in (p.xs, p.sp0, p.sp1, p.sp2, p.sp3):
                p.wait("Y", sem, sem.n)
            p.wait("Y", p.cw, p.cw.n)

        if g == 0:
            conv("wr")
            cv = conv("wo")
            p.wait("Y", p.cw, cv)
        p.wait("Y", p.T, st["Wfree0"])
        wr_tok = None
        for dc in range(DC):
            wr_tok = p.dma("Y", Wtile(0, dc),
                           s_wb["wr"][dc * 128:(dc + 1) * 128, :], rs=p.aw)
        for dc in range(DC):
            wr_tok = p.dma("Y", Wtile(0, 8 + dc),
                           s_wb["wo"][dc * 128:(dc + 1) * 128, :], rs=p.aw)
        p.wait("Y", p.V, st.get("accfree", 0))
        xr_tok = None
        for dc in range(DC):
            xr_tok = p.dma("Y", b_accb[:, dc * D:(dc + 1) * D],
                           s_xr[dc][:, g * GT:(g + 1) * GT], rs=p.xrl)

        p.wait("T", p.aw, wr_tok)
        p.wait("T", p.xrl, xr_tok)
        p.wait("T", p.T, st["hfree"])
        rs_tok = None
        for ec in range(DC):
            ps_wait([2, 3])
            mm = [None, None]
            for th in range(2):
                for dc in range(DC):
                    mm[th] = p.op(
                        "T",
                        lambda e, o=PS[2 + th][:, :],
                        l=Wtile(0, dc, ec * 128, 128),
                        r=b_accb[:, dc * D + th * 512: dc * D + th * 512 + 512],
                        s0=(dc == 0), s1=(dc == DC - 1):
                        e.matmul(o, l, r, start=s0, stop=s1),
                        inc=(dc == DC - 1))
            for th in range(2):
                p.wait("S", p.T, mm[th])
                rs_tok = p.op("S", lambda e, o=hs(8 + ec, 512, th * 512),
                              i=PS[2 + th][:, :]:
                              e.activation(o, i, AF.Sigmoid), inc=True)
                ps_used(2 + th, "S", rs_tok)
        st["xr_T_done"] = p.T.n

        p.phase = f"p2cg{g}_scan"
        fin_tok = None
        for dc in range(DC):
            bslot = dc % 2
            p.wait("Y", p.V, st.get(f"bflfree{bslot}", 0))
            bf = p.bflA if bslot == 0 else p.bflB
            p.dma("Y", b_bfl[:, bslot * 2 * D: bslot * 2 * D + D],
                  s_ekv[dc][:, g * GT:(g + 1) * GT], rs=bf)
            ek_l = p.dma("Y", b_bfl[:, bslot * 2 * D + D: (bslot + 1) * 2 * D],
                         s_ek[dc][:, g * GT:(g + 1) * GT], rs=bf)
            ekvap = b_bfl[:, bslot * 2 * D: bslot * 2 * D + D]
            ekap = b_bfl[:, bslot * 2 * D + D: (bslot + 1) * 2 * D]
            p.wait("V", bf, ek_l)
            p.op("V", lambda e, o=F(0, 1, 0), i=b_cin[:, dc:dc + 1]:
                 e.tensor_copy(o, i))
            p.op("V", lambda e, o=F(0, GT, 1),
                 d0=drv(1, dc).broadcast_to((128, GT)), d1=ekvap,
                 i0=b_cin[:, dc:dc + 1]:
                 e.tensor_tensor_scan(o, d0, d1, i0, ALU.mult, ALU.add))
            p.op("V", lambda e, o=b_cin[:, dc:dc + 1], i=F(0, 1, GT):
                 e.tensor_copy(o, i))
            p.op("V", lambda e, o=F(1, 1, 0), i=b_cin[:, 8 + dc:8 + dc + 1]:
                 e.tensor_copy(o, i))
            p.op("V", lambda e, o=F(1, GT, 1),
                 d0=drv(1, dc).broadcast_to((128, GT)), d1=ekap,
                 i0=b_cin[:, 8 + dc:8 + dc + 1]:
                 e.tensor_tensor_scan(o, d0, d1, i0, ALU.mult, ALU.add))
            p.op("V", lambda e, o=b_cin[:, 8 + dc:8 + dc + 1], i=F(1, 1, GT):
                 e.tensor_copy(o, i))
            p.op("V", lambda e, o=F(2), i=ekvap, s=drv(2, dc), a=F(0, GT, 0):
                 e.scalar_tensor_tensor(o, i, s, a, ALU.mult, ALU.add))
            dtok = p.op("V", lambda e, o=F(3), i=ekap, s=drv(2, dc),
                        a=F(1, GT, 0):
                        e.scalar_tensor_tensor(o, i, s, a, ALU.mult, ALU.add),
                        inc=True)
            st[f"bflfree{bslot}"] = dtok
            vtok = p.op("V", lambda e, o=F(4), i=F(3):
                        e.tensor_single_scalar(o, i, 1e-6, ALU.max), inc=True)
            p.wait("S", p.V, vtok)
            sln = p.op("S", lambda e, o=F(3), i=F(4):
                       e.activation(o, i, AF.Ln), inc=True)
            p.wait("S", p.S, sln)
            srd = p.op("S", lambda e, o=F(4), i=F(3):
                       e.activation(o, i, AF.Exp, scale=-1.0), inc=True)
            p.wait("V", p.S, srd)
            p.op("V", lambda e, o=F(2), a=F(2), b=F(4): e.tensor_mul(o, a, b))
            p.wait("V", p.S, rs_tok if dc == 0 else 0)
            fin_tok = p.op("V", lambda e, o=hs(dc, GT), a=F(2),
                           b=hs(8 + dc, GT): e.tensor_mul(o, a, b), inc=True)
        st["fin"] = fin_tok

        p.phase = f"p2cg{g}_wo"
        p.wait("T", p.V, fin_tok)
        tr3 = []
        for sub in range(nsub):
            par = sub % 2
            pr = PAIR[par]
            ps_wait(list(pr))
            mm = [None, None]
            for dc in range(DC):
                for eh in range(2):
                    mm[eh] = p.op(
                        "T",
                        lambda e, o=PS[pr[eh]][:, :],
                        l=hs(dc, 128, sub * 128),
                        r=Wtile(0, 8 + dc, eh * 512, 512),
                        s0=(dc == 0), s1=(dc == DC - 1):
                        e.matmul(o, l, r, start=s0, stop=s1),
                        inc=(dc == DC - 1))
            slot = b_x[:, (sub % 2) * D:(sub % 2 + 1) * D]
            row = g * GT + sub * 128
            p.wait("Y", p.V, st.get(f"xresfree{sub % 2}", 0))
            p.wait("Y", p.S, st.get(f"xresfreeS{sub % 2}", 0))
            p.waitt("Y", st.get(f"xresfreeG{sub % 2}"))
            p.waitt("Y", st.get(f"xstore{sub % 2}"))
            xl = p.xlA if sub % 2 == 0 else p.xlB
            ytok = p.dma("Y", slot, s_x1[row:row + 128, :], rs=xl)
            p.wait("V", xl, ytok)
            tok = None
            for eh in range(2):
                p.wait("V", p.T, mm[eh])
                tok = p.op("V", lambda e, o=slot[:, eh * 512:(eh + 1) * 512],
                           a=slot[:, eh * 512:(eh + 1) * 512],
                           b=PS[pr[eh]][:, :]: e.tensor_add(o, a, b), inc=True)
                ps_used(pr[eh], "V", tok)
            p.wait("Y", p.V, tok)
            p.dma("Y", s_x2[row:row + 128, :], slot)
            tr3.append(ln_transpose(sub, slot))
            st[f"xresfree{sub % 2}"] = p.V.n
            st[f"xresfreeS{sub % 2}"] = p.S.n
        st["Wfree0"] = p.T.n
        st["hfree"] = p.T.n
        return tr3

    def xin_row(g, sub):
        return g * GT + sub * 128 + (0 if g == 0 else 128)

    def own_row(g, sub):
        return g * GT + sub * 128 - (128 if g == 0 else 0)

    for g in range([0, NG if stage >= 2 else 1][stage >= 0.12]):
        p.phase = f"p1g{g}_head"
        nsub = 9 if g == 0 else 8
        if stage < 0.3:
            nsub = 1
        tr = []
        for sub in range(nsub):
            slot = b_x[:, (sub % 2) * D:(sub % 2 + 1) * D]
            p.wait("Y", p.V, st.get(f"xresfree{sub % 2}", 0))
            p.wait("Y", p.S, st.get(f"xresfreeS{sub % 2}", 0))
            p.waitt("Y", st.get(f"xstore{sub % 2}"))
            xl = p.xlA if sub % 2 == 0 else p.xlB
            ytok = p.dma("Y", slot,
                         d_xin[xin_row(g, sub): xin_row(g, sub) + 128, :],
                         rs=xl)
            p.wait("V", xl, ytok)
            p.wait("S", xl, ytok)
            if sub == 0 and g == 0:
                p.wait("T", p.V, st["bnd0"])
            tr.append(ln_transpose(sub, slot))
            st[f"xresfree{sub % 2}"] = p.V.n
            st[f"xresfreeS{sub % 2}"] = p.S.n

        tr2 = []

        def epi_p1(sub, tok, slot):
            if stage < 1.1:
                st[f"xresfree{sub % 2}"] = tok
                return
            p.wait("S", p.V, tok)
            if stage >= 1.145:
                parts = "full"
            elif stage >= 1.1405:
                nfull = {1.141: 1, 1.142: 2, 1.143: 4, 1.144: 6}.get(stage, 0)
                parts = "full" if sub < nfull else "tr"
            elif stage >= 1.14:
                parts = "tr"
            else:
                parts = "ln"
            tr2.append(ln_transpose(sub, slot, parts=parts))
            st[f"xresfree{sub % 2}"] = p.V.n
            st[f"xresfreeS{sub % 2}"] = p.S.n

        res = ffn_core(
            "p1", g,
            lambda sub, g=g: d_xin[xin_row(g, sub): xin_row(g, sub) + 128, :],
            lambda sub, g=g: None if (g == 0 and sub == 0)
            else s_x1[own_row(g, sub): own_row(g, sub) + 128, :],
            tr, epi=epi_p1)
        if stage >= 1.1 and res:
            p1_tail(g, tr2)

    p.phase = "xchg"
    if stage >= 3:
        stg = p.op("V", lambda e, o=b_stage[:, :], i=b_ends[:, :],
                   s=b_mask[:, 0:1]: e.tensor_scalar_mul(o, i, s), inc=True)
        p.wait("G", p.V, stg)
        gtok = p.dma("G", cc_in[:, :], b_stage[:, :], rs=p.cc2)
        p.wait("G", p.cc2, gtok)
        p.CC.n += 1
        p.st["G"].append((p.phase,
            lambda e: e.collective_compute(
                "AllReduce", ALU.add,
                replica_groups=[[0, 1], [2, 3], [4, 5], [6, 7]],
                ins=[cc_in.ap().opt()], outs=[cc_out.ap().opt()],
            ).then_inc(p.CC.h, 1)))
        p.wait("G", p.CC, p.CC.n)
        cin_tok = p.dma("G", b_cin[:, :], cc_out[:, :], rs=p.cc2)
        p.wait("V", p.cc2, cin_tok)
        cin_v = p.op("V", lambda e, o=b_cin[:, :], i=b_cin[:, :],
                     s=b_mask[:, 1:2]: e.tensor_scalar_mul(o, i, s), inc=True)
        st["cin"] = cin_v

    if stage >= 4:
        for g in range(NG if stage >= 5 else 1):
            tr3 = p2c_group(g)
            res = ffn_core(
                "p4", g,
                lambda sub, g=g: s_x2[g * GT + sub * 128:
                                      g * GT + sub * 128 + 128, :],
                lambda sub, g=g: d_y[g * GT + sub * 128:
                                     g * GT + sub * 128 + 128, :],
                tr3, store_q="G")
            st["out_last"] = 1

    if debug:
        for r0 in range(0, TOWN, 1024):
            p.dma("G", dbg_x1[r0:r0 + 1024, :], s_x1[r0:r0 + 1024, :])
            p.dma("G", dbg_x2[r0:r0 + 1024, :], s_x2[r0:r0 + 1024, :])
        p.dma("G", dbg_sm[:, 0:16], b_ends[:, :])
        p.dma("G", dbg_sm[:, 16:32], b_cin[:, :])
        p.dma("G", dbg_sm[:, 32:36], b_bnd[:, 0:8].bitcast(F32))
        tok = p.dma("G", dbg_ekv[:, 0:2048], s_ekv[0].bitcast(F32))
        st["out_last"] = max(st.get("out_last", 0), tok)

    for s_, sems in (
        ("Y", (p.Y, p.ct, p.cw, p.winS0, p.winS1, p.aw, p.xrl, p.bflA,
               p.bflB, p.xlA, p.xlB, p.xs, p.stA, p.stB)),
        ("G", (p.G, p.wg, p.cr, p.sp0, p.sp1, p.sp2, p.sp3, p.cc2,
               p.oA, p.oB)),
        ("S", (p.SQ, p.wq0, p.wq1)),
    ):
        for sem in sems:
            p.wait(s_, sem, sem.n)

    _blk = nc.Block()
    block = _blk.__enter__()
    for name, attr in (("Y", "sync"), ("G", "gpsimd"), ("T", "tensor"),
                       ("V", "vector"), ("S", "scalar")):
        def run_ops(eng, ops=p.st[name]):
            cur = [None, None]

            def close():
                if cur[1] is not None:
                    cur[1].__exit__(None, None, None)
                    cur[1] = None
            for ph, f in ops:
                if ph != cur[0]:
                    close()
                    cur[0] = ph
                    cur[1] = nc.named_scope(ph)
                    cur[1].__enter__()
                f(eng)
            close()
        getattr(block, attr)(run_ops)
    _blk.__exit__(None, None, None)
    return nc


def _stage_inputs(inputs):
    x = np.asarray(inputs["L0_attn_x"], np.float32)
    win_t = np.ascontiguousarray(np.asarray(inputs["L0_ffn_w_in"],
                                            np.float32).T)
    w1_t = np.ascontiguousarray(np.asarray(inputs["L0_ffn_w1"], np.float32).T)
    wout_t = np.ascontiguousarray(np.asarray(inputs["L0_ffn_w_out"],
                                             np.float32).T)
    w2_t = np.ascontiguousarray(
        np.asarray(inputs["L0_ffn_w2"], np.float32).transpose(0, 2, 1))
    wge_t = np.ascontiguousarray(np.concatenate(
        [np.asarray(inputs["L0_ffn_wg"], np.float32).T,
         np.asarray(inputs["L0_ffn_we"], np.float32).T], axis=1))
    gbeb = np.ascontiguousarray(np.broadcast_to(np.concatenate(
        [np.asarray(inputs["L0_ffn_gb"], np.float32),
         np.asarray(inputs["L0_ffn_eb"], np.float32)]), (128, 10)))
    wk_t = np.ascontiguousarray(np.asarray(inputs["L0_attn_Wk"], np.float32).T)
    wv_t = np.ascontiguousarray(np.asarray(inputs["L0_attn_Wv"], np.float32).T)
    wr_t = np.ascontiguousarray(np.asarray(inputs["L0_attn_Wr"], np.float32).T)
    wo_t = np.ascontiguousarray(np.asarray(inputs["L0_attn_Wo"], np.float32).T)
    tvec = np.ascontiguousarray(np.stack([
        np.asarray(inputs["L0_attn_time_decay"], np.float32),
        np.asarray(inputs["L0_attn_time_first"], np.float32),
        np.asarray(inputs["L0_attn_time_mix_k"], np.float32),
        np.asarray(inputs["L0_attn_time_mix_v"], np.float32),
        np.asarray(inputs["L0_attn_time_mix_r"], np.float32),
    ]).reshape(5, DC, 128, 1))
    ident = np.ascontiguousarray(np.eye(128, dtype=bfloat16))

    in_maps = []
    for c in range(8):
        b, h = c // 2, c % 2
        if h == 0:
            prev = np.zeros((128, D), np.float32)
            own = x[b, 0:TOWN]
        else:
            prev = x[b, TOWN - 128:TOWN]
            own = x[b, TOWN:2 * TOWN]
        xin = np.ascontiguousarray(np.concatenate([prev, own], axis=0))
        masks = np.zeros((128, 2), np.float32)
        masks[:, 0] = 1.0 if h == 0 else 0.0
        masks[:, 1] = 1.0 if h == 1 else 0.0
        in_maps.append({
            "xin": xin, "win_t": win_t, "w1_t": w1_t, "wout_t": wout_t,
            "w2_t": w2_t, "wge_t": wge_t, "gbeb": gbeb,
            "wk_t": wk_t, "wv_t": wv_t, "wr_t": wr_t, "wo_t": wo_t,
            "tvec": tvec, "masks": masks, "ident": np.asarray(ident),
        })
    return in_maps


def kernel(**inputs):
    if "nc" not in _CACHE:
        _CACHE["nc"] = build_program()
    nc = _CACHE["nc"]
    in_maps = _stage_inputs(inputs)
    res = run_bass_kernel_spmd(nc, in_maps, core_ids=list(range(8)))
    out = np.empty((4, 2 * TOWN, D), np.float32)
    for c in range(8):
        b, h = c // 2, c % 2
        out[b, h * TOWN:(h + 1) * TOWN] = res.results[c]["y"]
    return out


# version 05 (final)
# speedup vs baseline: 1.2170x; 1.1146x over previous
import numpy as np
import ml_dtypes
import concourse.bass as bass
import concourse.mybir as mybir
from concourse.bass_utils import run_bass_kernel_spmd

F32 = mybir.dt.float32
BF16 = mybir.dt.bfloat16
AF = mybir.ActivationFunctionType
ALU = mybir.AluOpType
AX = mybir.AxisListType
bfloat16 = ml_dtypes.bfloat16

D = 1024
DC = 8
HID = 2048
NJC = 16
NE = 8
TOWN = 4096
NG = 4
GT = 1024
GW0 = GT + 128

_CACHE = {}
SERIAL_DMA = False


class Sem:
    def __init__(self, nc, name):
        self.h = nc.alloc_semaphore(name)
        self.n = 0


class Prog:
    def __init__(self, nc):
        self.nc = nc
        self.phase = "init"
        self.st = {"T": [], "V": [], "S": [], "Y": [], "G": []}
        self.T = Sem(nc, "sT")
        self.V = Sem(nc, "sV")
        self.S = Sem(nc, "sS")
        self.Y = Sem(nc, "sY")
        self.G = Sem(nc, "sG")
        self.SQ = Sem(nc, "sSQ")
        self.CC = Sem(nc, "sCC")
        for nm in ("ct", "cw", "cr", "wg", "xlA", "xlB", "winS0", "winS1",
                   "aw", "xrl", "bflA", "bflB", "xs", "sp0", "sp1", "sp2",
                   "sp3", "cc2", "oA", "oB", "wq0", "wq1", "stA", "stB",
                   "spw"):
            setattr(self, nm, Sem(nc, "s_" + nm))

    def _app(self, s, fn):
        self.st[s].append((self.phase, fn))

    def wait(self, s, sem, tok):
        if tok and tok > 0:
            self._app(s, lambda e, h=sem.h, t=tok: e.wait_ge(h, t))

    def waitt(self, s, t):
        if t:
            sem, v = t
            self.wait(s, sem, v)

    def op(self, s, fn, inc=False):
        if s in ("V", "S"):
            sem = getattr(self, s)
            self.wait(s, sem, sem.n)
            sem.n += 1
            self._app(s, lambda e, f=fn, h=sem.h: f(e).then_inc(h, 1))
            return sem.n
        if inc:
            sem = getattr(self, s)
            sem.n += 1
            self._app(s, lambda e, f=fn, h=sem.h: f(e).then_inc(h, 1))
            return sem.n
        self._app(s, fn)
        return None

    def dma(self, s, dst, src, rs=None):
        sem = rs if rs is not None else {"Y": self.Y, "G": self.G,
                                         "S": self.SQ}[s]
        sem.n += 16
        self._app(
            s, lambda e, d=dst, i=src, h=sem.h:
            e.dma_start(out=d, in_=i).then_inc(h, 16))
        if SERIAL_DMA:
            tok = sem.n
            self._app(s, lambda e, h=sem.h, t=tok: e.wait_ge(h, t))
        return sem.n


def build_program(stage=5, debug=False):
    nc = bass.Bass("TRN2", target_bir_lowering=False, debug=False)
    p = Prog(nc)

    def din(name, shape, dt=F32):
        return nc.dram_tensor(name, list(shape), dt, kind="ExternalInput").ap()

    d_xin = din("xin", [128 + TOWN, D])
    d_win = din("win_t", [D, 2 * HID])
    d_w1 = din("w1_t", [D, 2 * HID])
    d_wout = din("wout_t", [HID, D])
    d_w2 = din("w2_t", [NE, HID, D])
    d_wge = din("wge_t", [D, 10])
    d_gbeb = din("gbeb", [128, 10])
    d_wk = din("wk_t", [D, D])
    d_wv = din("wv_t", [D, D])
    d_wr = din("wr_t", [D, D])
    d_wo = din("wo_t", [D, D])
    d_tvec = din("tvec", [5, DC, 128, 1])
    d_mask = din("masks", [128, 2])
    d_ident = din("ident", [128, 128], BF16)
    d_y = nc.dram_tensor("y", [TOWN, D], F32, kind="ExternalOutput").ap()
    if debug:
        dbg_x1 = nc.dram_tensor("dbg_x1", [TOWN, D], F32,
                                kind="ExternalOutput").ap()
        dbg_x2 = nc.dram_tensor("dbg_x2", [TOWN, D], F32,
                                kind="ExternalOutput").ap()
        dbg_sm = nc.dram_tensor("dbg_sm", [128, 48], F32,
                                kind="ExternalOutput").ap()
        dbg_ekv = nc.dram_tensor("dbg_ekv", [128, TOWN], F32,
                                 kind="ExternalOutput").ap()
        dbg_xlnT = nc.dram_tensor("dbg_xlnT", [128, GW0], BF16,
                                  kind="ExternalOutput").ap()
        dbg_h = nc.dram_tensor("dbg_h", [128, GW0], BF16,
                               kind="ExternalOutput").ap()
        dbg_acc = nc.dram_tensor("dbg_acc", [128, D], F32,
                                 kind="ExternalOutput").ap()
        dbg_wse = nc.dram_tensor("dbg_wse", [128, 72], F32,
                                 kind="ExternalOutput").ap()
        dbg_w = nc.dram_tensor("dbg_w", [128, 4096], BF16,
                               kind="ExternalOutput").ap()
        dbg_rt = nc.dram_tensor("dbg_rt", [128, 9 * 64], F32,
                                kind="ExternalOutput").ap()

    def scr(name, shape, dt):
        return nc.dram_tensor(name, list(shape), dt, kind="Internal").ap()

    s_wb = {
        "win": scr("winb", [D, 2 * HID], BF16),
        "w1": scr("w1b", [D, 2 * HID], BF16),
        "wout": scr("woutb", [HID, D], BF16),
        "wk": scr("wkb", [D, D], BF16),
        "wv": scr("wvb", [D, D], BF16),
        "wr": scr("wrb", [D, D], BF16),
        "wo": scr("wob", [D, D], BF16),
    }
    for e_ in range(NE):
        s_wb[f"w2_{e_}"] = scr(f"w2b{e_}", [HID, D], BF16)
    s_x1 = scr("x1s", [TOWN, D], F32)
    s_x2 = scr("x2s", [TOWN, D], F32)
    s_xr = scr("xrs", [DC, 128, TOWN], BF16)
    s_ekv = scr("ekvs", [DC, 128, TOWN], BF16)
    s_ek = scr("eks", [DC, 128, TOWN], BF16)
    s_wkv = scr("wkvs", [DC, 128, TOWN], BF16)
    cc_in = nc.dram_tensor("cc_in", [128, 16], F32, kind="Internal")
    cc_out = nc.dram_tensor("cc_out", [128, 16], F32, kind="Internal")

    def sb(name, shape, dt):
        return nc.alloc_sbuf_tensor(name, list(shape), dt)

    b_x = sb("b_x", [128, 2 * D], F32)
    b_xln = sb("b_xln", [128, 2 * D], BF16)
    b_xlnT = sb("b_xlnT", [128, DC * GW0], BF16)
    b_h = sb("b_h", [128, NJC * GW0], BF16)
    b_acc = sb("b_acc", [128, 9 * D], F32)
    b_accb = b_acc.ap().bitcast(BF16)
    b_win = sb("b_win", [128, 2 * DC * 256], BF16)
    b_winf = b_win.ap().bitcast(F32)
    b_W = sb("b_W", [128, 2 * 16 * D], BF16)
    b_F = sb("b_F", [128, 5 * 1032], F32)
    b_bfl = sb("b_bfl", [128, 4 * D], BF16)
    b_conv = sb("b_conv", [128, 1024], BF16)
    b_wge = sb("b_wge", [128, DC * 10], BF16)
    b_gbeb = sb("b_gbeb", [128, 10], F32)
    b_tv = sb("b_tv", [128, 5 * DC], F32)
    b_drv = sb("b_drv", [128, 3 * DC], F32)
    b_mask = sb("b_mask", [128, 2], F32)
    b_id = sb("b_id", [128, 128], BF16)
    b_ends = sb("b_ends", [128, 16], F32)
    b_cin = sb("b_cin", [128, 16], F32)
    b_stage = sb("b_stage", [128, 16], F32)
    b_bnd = sb("b_bnd", [128, DC], BF16)
    b_wse = sb("b_wse", [128, 9 * 8], F32)
    b_rt = sb("b_rt", [128, 9 * 64], F32)
    b_st = sb("b_st", [128, 24], F32)

    PS0T = nc.alloc_psum_tensor("PS0", [128, 1024], BF16)
    PS = [PS0T] + [nc.alloc_psum_tensor(f"PS{i}", [128, 512], F32)
                   for i in range(1, 8)]
    PS0b = PS0T.ap()

    def F(i, w=1024, off=0):
        return b_F[:, i * 1032 + off: i * 1032 + off + w]

    def Wtile(slot, i, c0=0, cw=D):
        base = slot * 16 * D + i * D
        return b_W[:, base + c0: base + c0 + cw]

    def hs(j, w, off=0):
        return b_h[:, j * GW0 + off: j * GW0 + off + w]

    def xlnT(dc, off, w):
        return b_xlnT[:, dc * GW0 + off: dc * GW0 + off + w]

    def tvcol(v, dc):
        return b_tv[:, v * DC + dc: v * DC + dc + 1]

    def drv(v, dc):
        return b_drv[:, v * DC + dc: v * DC + dc + 1]

    st = {}

    def ps_wait(idxs):
        for i in (idxs if isinstance(idxs, (list, tuple)) else [idxs]):
            p.wait("T", p.V, st.get(f"psv{i}", 0))
            p.wait("T", p.S, st.get(f"pss{i}", 0))

    def ps_used(i, eng, tok):
        st[f"ps{'v' if eng == 'V' else 's'}{i}"] = tok

    PAIR = {0: (2, 3), 1: (5, 6)}
    TRIPLE = {0: (2, 3, 4), 1: (5, 6, 7)}

    p.phase = "pro"
    t_cst = 0
    for dst, src in [(b_gbeb[:, :], d_gbeb), (b_mask[:, :], d_mask),
                     (b_id[:, :], d_ident)]:
        t_cst = p.dma("Y", dst, src, rs=p.ct)
    for v in range(5):
        for dc in range(DC):
            t_cst = p.dma("Y", tvcol(v, dc), d_tvec[v, dc], rs=p.ct)
    t_wge = 0
    for dc in range(DC):
        t_wge = p.dma("G", b_wge[:, dc * 10:(dc + 1) * 10],
                      d_wge[dc * 128:(dc + 1) * 128, :], rs=p.wg)
    st["wge"] = t_wge

    p.wait("S", p.ct, t_cst)
    t_drv = 0
    for dc in range(DC):
        p.op("S", lambda e, o=drv(0, dc), i=tvcol(0, dc):
             e.activation(o, i, AF.Exp))
        p.op("S", lambda e, o=drv(0, dc), i=drv(0, dc): e.mul(o, i, -1.0))
        p.op("S", lambda e, o=drv(1, dc), i=drv(0, dc):
             e.activation(o, i, AF.Exp))
        t_drv = p.op("S", lambda e, o=drv(2, dc), i=tvcol(1, dc):
                     e.activation(o, i, AF.Exp), inc=True)
    st["drv"] = t_drv
    p.wait("V", p.S, t_drv)
    p.op("V", lambda e: e.memset(b_ends[:, :], 0.0))
    st["bnd0"] = p.op("V", lambda e: e.memset(b_bnd[:, :], 0.0), inc=True)

    conv_done = {}
    conv_state = {"war": 0}
    conv_pieces = {
        "win": (d_win, s_wb["win"], D, 2 * HID),
        "w1": (d_w1, s_wb["w1"], D, 2 * HID),
        "wout": (d_wout, s_wb["wout"], HID, D),
        "wk": (d_wk, s_wb["wk"], D, D),
        "wv": (d_wv, s_wb["wv"], D, D),
        "wr": (d_wr, s_wb["wr"], D, D),
        "wo": (d_wo, s_wb["wo"], D, D),
    }
    for e_ in range(NE):
        conv_pieces[f"w2_{e_}"] = (d_w2[e_], s_wb[f"w2_{e_}"], HID, D)

    conv_slots = [b_conv[:, 0:1024]] + [
        b_bfl[:, i * 1024:(i + 1) * 1024] for i in range(4)]
    conv_war = [0] * 5

    def conv(name):
        if name in conv_done:
            return conv_done[name]
        src, dst, rows, cols = conv_pieces[name]
        tok_y = 0
        sl = conv_state.get("sl", 0)
        for r0 in range(0, rows, 128):
            for c0 in range(0, cols, 1024):
                w = min(1024, cols - c0)
                cb = conv_slots[sl][:, 0:w]
                p.wait("G", p.cw, conv_war[sl])
                tg = p.dma("G", cb, src[r0:r0 + 128, c0:c0 + w], rs=p.cr)
                p.wait("Y", p.cr, tg)
                tok_y = p.dma("Y", dst[r0:r0 + 128, c0:c0 + w], cb, rs=p.cw)
                conv_war[sl] = tok_y
                sl = (sl + 1) % 5
        conv_state["sl"] = sl
        conv_done[name] = tok_y
        return tok_y

    def ln_transpose(sub, xap, parts="full"):
        xlnap = b_xln[:, (sub % 2) * D:(sub % 2 + 1) * D]
        p.wait("V", p.T, st.get(f"xlnfree{sub % 2}", 0))
        par_ = sub % 2
        stt = b_st[:, par_ * 8:par_ * 8 + 8]
        p.wait("V", p.S, st.get(f"lnstatS{par_}", 0))
        p.op("V", lambda e, o=stt[:, 0:1], i=xap: e.reduce_sum(o, i, axis=AX.X))
        p.wait("S", p.T, st.get("winfree0", 0))
        tok_s = p.op("S", lambda e, o=b_winf[:, 0:D], i=xap,
                     a=stt[:, 1:2]:
                     e.activation(o, i, AF.Square, accum_out=a), inc=True)
        st["sqtok"] = tok_s
        tok_m = p.op("V", lambda e, o=stt[:, 2:3], i=stt[:, 0:1]:
                     e.tensor_scalar_mul(o, i, 1.0 / D), inc=True)
        p.wait("S", p.V, tok_m)
        tok_s2 = p.op("S", lambda e, o=stt[:, 3:4], i=stt[:, 2:3]:
                      e.activation(o, i, AF.Square), inc=True)
        p.wait("V", p.S, tok_s)
        p.op("V", lambda e, o=stt[:, 4:5], i=stt[:, 1:2]:
             e.tensor_scalar_mul(o, i, 1.0 / D))
        p.wait("V", p.S, tok_s2)
        p.op("V", lambda e, o=stt[:, 4:5], a=stt[:, 4:5], b=stt[:, 3:4]:
             e.tensor_sub(o, a, b))
        tok_v = p.op("V", lambda e, o=stt[:, 4:5], i=stt[:, 4:5]:
                     e.tensor_scalar_add(o, i, 1e-5), inc=True)
        p.wait("S", p.V, tok_v)
        tok_s = p.op("S", lambda e, o=stt[:, 5:6], i=stt[:, 4:5]:
                     e.activation(o, i, AF.Sqrt), inc=True)
        st[f"lnstatS{par_}"] = tok_s
        p.wait("V", p.S, tok_s)
        p.op("V", lambda e, o=stt[:, 6:7], i=stt[:, 5:6]: e.reciprocal(o, i))
        ln_tok = p.op("V", lambda e, o=xlnap, i=xap, m=stt[:, 2:3],
                      r=stt[:, 6:7]:
                      e.tensor_scalar(o, i, m, r, ALU.subtract, ALU.mult),
                      inc=True)
        if stage < 0.18 or parts == "ln":
            return (ln_tok, 1)
        p.wait("T", p.V, ln_tok)
        ps_wait(0)
        ttok = None
        for dc in range(DC):
            ttok = p.op("T", lambda e, o=PS0b[:, dc * 128:(dc + 1) * 128],
                        i=xlnap[:, dc * 128:(dc + 1) * 128], idn=b_id[:, :]:
                        e.transpose(o, i, idn), inc=(dc == DC - 1))
        st[f"xlnfree{sub % 2}"] = ttok
        if stage < 0.22 or parts == "tr":
            return (ttok, 1)
        vt = stk = None
        for dc in range(DC):
            s = "V"
            p.wait(s, p.T, ttok)
            if s == "V":
                tok = p.op(s, lambda e, o=xlnT(dc, sub * 128, 128),
                           i=PS0b[:, dc * 128:(dc + 1) * 128]:
                           e.tensor_copy(o, i), inc=True)
            else:
                tok = p.op(s, lambda e, o=xlnT(dc, sub * 128, 128),
                           i=PS0b[:, dc * 128:(dc + 1) * 128]:
                           e.copy(o, i), inc=True)
            if s == "V":
                vt = tok
            else:
                stk = tok
        ps_used(0, "V", vt)
        ps_used(0, "S", stk)
        return vt, stk

    def ffn_core(pass_id, g, x_resid_src, store_dst, tr_toks,
                 store_q="Y", epi=None, weave=None, at_e6=None):
        if stage < 0.5:
            return []
        nsub = 9 if (g == 0 and pass_id == "p1") else 8
        w = nsub * 128
        tsp = ([(0, 128), (128, 512), (640, 512)] if w == GW0
               else [(0, 512), (512, 512)])
        first = pass_id == "p1" and g == 0

        p.phase = f"{pass_id}g{g}_gate"
        for sub in range(nsub):
            vt, stk = tr_toks[sub]
            p.wait("T", p.V, vt)
            p.wait("T", p.S, stk)
            if sub == 0:
                p.wait("T", p.wg, st["wge"])
            p.wait("T", p.V, st.get("gatefree", 0))
            ps = PS[1][:, 0:10]
            gtok = None
            for dc in range(DC):
                gtok = p.op("T", lambda e, o=ps,
                            l=xlnT(dc, sub * 128, 128),
                            r=b_wge[:, dc * 10:(dc + 1) * 10],
                            s0=(dc == 0), s1=(dc == DC - 1):
                            e.matmul(o, l, r, start=s0, stop=s1),
                            inc=(dc == DC - 1))
            p.wait("V", p.T, gtok)
            tok = p.op("V", lambda e, o=b_rt[:, sub * 64: sub * 64 + 10],
                       a=ps, b=b_gbeb[:, :]: e.tensor_add(o, a, b), inc=True)
            st["gatefree"] = tok
        ps_used(1, "V", st["gatefree"])

        def rtc(sub, c0, c1=None):
            c1 = c0 + 1 if c1 is None else c1
            return b_rt[:, sub * 64 + c0: sub * 64 + c1]

        vtok = None
        for sub in range(nsub):
            p.op("V", lambda e, o=rtc(sub, 10), a=rtc(sub, 0), b=rtc(sub, 1):
                 e.tensor_sub(o, a, b))
            p.op("V", lambda e, o=rtc(sub, 11), i=rtc(sub, 10):
                 e.tensor_single_scalar(o, i, 0.0, ALU.is_ge))
            p.op("V", lambda e, o=rtc(sub, 12), i=rtc(sub, 2, 6):
                 e.reduce_max(o, i, axis=AX.X))
            p.op("V", lambda e, o=rtc(sub, 13), i=rtc(sub, 6, 10):
                 e.reduce_max(o, i, axis=AX.X))
            p.op("V", lambda e, o=rtc(sub, 12), i=rtc(sub, 12):
                 e.tensor_scalar_mul(o, i, -1.0))
            vtok = p.op("V", lambda e, o=rtc(sub, 13), i=rtc(sub, 13):
                        e.tensor_scalar_mul(o, i, -1.0), inc=True)
        p.wait("S", p.V, vtok)
        stok = None
        for sub in range(nsub):
            p.op("S", lambda e, o=rtc(sub, 14), i=rtc(sub, 10):
                 e.activation(o, i, AF.Abs))
            p.op("S", lambda e, o=rtc(sub, 15), i=rtc(sub, 14):
                 e.activation(o, i, AF.Sigmoid))
        for sub in range(nsub):
            p.op("S", lambda e, o=rtc(sub, 18, 22), i=rtc(sub, 2, 6),
                 b=rtc(sub, 12), a=rtc(sub, 26):
                 e.activation(o, i, AF.Exp, bias=b, accum_out=a))
            stok = p.op("S", lambda e, o=rtc(sub, 22, 26), i=rtc(sub, 6, 10),
                        b=rtc(sub, 13), a=rtc(sub, 27):
                        e.activation(o, i, AF.Exp, bias=b, accum_out=a),
                        inc=True)
        p.wait("V", p.S, stok)
        wse_tok = None
        for sub in range(nsub):
            p.op("V", lambda e, o=rtc(sub, 28), i=rtc(sub, 26):
                 e.reciprocal(o, i))
            p.op("V", lambda e, o=rtc(sub, 29), i=rtc(sub, 27):
                 e.reciprocal(o, i))
            p.op("V", lambda e, o=rtc(sub, 16), a=rtc(sub, 15), b=rtc(sub, 11):
                 e.tensor_mul(o, a, b))
            p.op("V", lambda e, o=rtc(sub, 17), a=rtc(sub, 15), b=rtc(sub, 16):
                 e.tensor_sub(o, a, b))
            p.op("V", lambda e, o=rtc(sub, 30), a=rtc(sub, 16), b=rtc(sub, 28):
                 e.tensor_mul(o, a, b))
            p.op("V", lambda e, o=rtc(sub, 31), a=rtc(sub, 17), b=rtc(sub, 29):
                 e.tensor_mul(o, a, b))
            p.op("V", lambda e, o=rtc(sub, 32, 36), i=rtc(sub, 18, 22),
                 s=rtc(sub, 30): e.tensor_scalar_mul(o, i, s))
            p.op("V", lambda e, o=rtc(sub, 36, 40), i=rtc(sub, 22, 26),
                 s=rtc(sub, 31): e.tensor_scalar_mul(o, i, s))
            p.op("V", lambda e, o=rtc(sub, 40), i=rtc(sub, 32, 40):
                 e.reduce_max(o, i, axis=AX.X))
            p.op("V", lambda e, o=rtc(sub, 41, 49), i=rtc(sub, 32, 40),
                 s=rtc(sub, 40): e.tensor_single_scalar(o, i, s, ALU.is_ge))
            p.op("V", lambda e, o=rtc(sub, 49, 57), i=rtc(sub, 41, 49),
                 b=rtc(sub, 32, 40):
                 e.scalar_tensor_tensor(o, i, -2.0, b, ALU.mult, ALU.add))
            p.op("V", lambda e, o=rtc(sub, 57), i=rtc(sub, 49, 57):
                 e.reduce_max(o, i, axis=AX.X))
            p.op("V", lambda e, o=rtc(sub, 41, 49), i=rtc(sub, 32, 40),
                 s=rtc(sub, 57): e.tensor_single_scalar(o, i, s, ALU.is_ge))
            p.op("V", lambda e, o=rtc(sub, 49, 57), a=rtc(sub, 41, 49),
                 b=rtc(sub, 32, 40): e.tensor_mul(o, a, b))
            wse_tok = p.op("V", lambda e, o=b_wse[:, sub * 8:(sub + 1) * 8],
                           i=rtc(sub, 49, 57):
                           e.tensor_scalar_mul(o, i, 0.5), inc=True)

        if stage < 0.6:
            return []
        jblk_tok = [None] * 32

        def emit_jblk(k):
            if k >= 32 or jblk_tok[k] is not None:
                return
            wname = "win" if k < 16 else "w1"
            jb = k % 16
            slot = k % 2
            base = slot * DC * 256
            if first:
                conv(wname)
                p.wait("Y", p.cw, conv_done[wname])
            p.wait("Y", p.T, st.get(f"winfree{slot}", 0))
            if slot == 0:
                p.wait("Y", p.S, st.get("sqtok", 0))
            ws = p.winS0 if slot == 0 else p.winS1
            tok = None
            for dc in range(DC):
                tok = p.dma(
                    "Y", b_win[:, base + dc * 256: base + (dc + 1) * 256],
                    s_wb[wname][dc * 128:(dc + 1) * 128,
                                jb * 256:(jb + 1) * 256], rs=ws)
            jblk_tok[k] = (ws, tok)

        emit_jblk(0)
        emit_jblk(1)

        def load_W(slot, wname, war_tok):
            if first:
                cv = conv(wname)
                p.wait("S", p.cw, cv)
            p.wait("S", p.T, war_tok)
            wq = p.wq0 if slot == 0 else p.wq1
            tok = None
            for i in range(16):
                tok = p.dma("S", Wtile(slot, i),
                            s_wb[wname][i * 128:(i + 1) * 128, :], rs=wq)
            return (wq, tok)

        wout_tok = load_W(0, "wout", st.get("Wfree0", 0))
        w2_tok = {0: load_W(1, "w2_0", st.get("Wfree1", 0))}

        def h_prod(m_i):
            silu_toks = [None] * NJC
            last_v = None
            for jc in range(2 * NJC):
                k = m_i * 16 + jc // 2
                slot = k % 2
                base = slot * DC * 256 + (jc % 2) * 128
                p.waitt("T", jblk_tok[k])
                if jc == 0:
                    p.wait("T", p.xs, st.get("hfreeY", 0))
                is_a = jc < NJC
                hj = jc if is_a else jc - NJC
                par = jc % 2
                tiles = TRIPLE[par][:len(tsp)]
                ps_wait(list(tiles))
                mms = []
                for ci, (t0, tn) in enumerate(tsp):
                    mm = None
                    for dc in range(DC):
                        mm = p.op(
                            "T",
                            lambda e, o=PS[tiles[ci]][:, 0:tn],
                            l=b_win[:, base + dc * 256: base + dc * 256 + 128],
                            r=xlnT(dc, t0, tn),
                            s0=(dc == 0), s1=(dc == DC - 1):
                            e.matmul(o, l, r, start=s0, stop=s1),
                            inc=(dc == DC - 1))
                    mms.append(mm)
                if jc % 2 == 1:
                    st[f"winfree{slot}"] = mms[-1]
                    emit_jblk(k + 2)
                if is_a:
                    tok = None
                    for ci, (t0, tn) in enumerate(tsp):
                        p.wait("S", p.T, mms[ci])
                        tok = p.op("S", lambda e, o=hs(hj, tn, t0),
                                   i=PS[tiles[ci]][:, 0:tn]:
                                   e.activation(o, i, AF.Silu), inc=True)
                        ps_used(tiles[ci], "S", tok)
                    silu_toks[hj] = tok
                else:
                    p.wait("V", p.S, silu_toks[hj])
                    for ci, (t0, tn) in enumerate(tsp):
                        p.wait("V", p.T, mms[ci])
                        tok = p.op("V", lambda e, o=hs(hj, tn, t0),
                                   a=hs(hj, tn, t0), b=PS[tiles[ci]][:, 0:tn]:
                                   e.tensor_mul(o, a, b), inc=True)
                        ps_used(tiles[ci], "V", tok)
                    last_v = tok
            return last_v

        if pass_id == "p4":
            p.wait("S", p.V, st.get("fin", 0))
        if stage < 0.7:
            return []
        p.phase = f"{pass_id}g{g}_h1"
        h1_v = h_prod(0)
        if debug and pass_id == "p1" and g == 0:
            p.wait("G", p.V, h1_v)
            p.dma("G", dbg_xlnT[:, :], b_xlnT[:, 0:GW0])
            p.dma("G", dbg_h[:, :], b_h[:, 0:GW0])
            p.dma("G", dbg_wse[:, :], b_wse[:, :])
            p.dma("G", dbg_rt[:, :], b_rt[:, :])
            dtok = p.dma("G", dbg_w[:, :], s_wb["win"][0:128, :])
            p.wait("V", p.G, dtok)
            p.wait("S", p.G, dtok)
            p.wait("T", p.G, dtok)
        if stage < 0.8:
            return []

        p.phase = f"{pass_id}g{g}_sh"
        p.waitt("T", wout_tok)
        p.wait("T", p.V, h1_v)
        if pass_id == "p4":
            p.wait("V", p.T, st.get("xr_T_done", 0))
        for sub in range(nsub):
            par = sub % 2
            pr = PAIR[par]
            ps_wait(list(pr))
            mms = [None, None]
            for jc in range(NJC):
                for eh in range(2):
                    mms[eh] = p.op(
                        "T",
                        lambda e, o=PS[pr[eh]][:, :],
                        l=hs(jc, 128, sub * 128),
                        r=Wtile(0, jc, eh * 512, 512),
                        s0=(jc == 0), s1=(jc == NJC - 1):
                        e.matmul(o, l, r, start=s0, stop=s1),
                        inc=(jc == NJC - 1))
            for eh in range(2):
                p.wait("V", p.T, mms[eh])
                tok = p.op("V", lambda e,
                           o=b_acc[:, sub * D + eh * 512:
                                   sub * D + eh * 512 + 512],
                           i=PS[pr[eh]][:, :]:
                           e.tensor_scalar_mul(o, i, 0.5), inc=True)
                ps_used(pr[eh], "V", tok)
        wout_T_done = p.T.n
        if debug and pass_id == "p1" and g == 0:
            p.wait("G", p.V, p.V.n)
            dtok = p.dma("G", dbg_acc[:, :], b_acc[:, 0:D])
            p.wait("V", p.G, dtok)
            p.wait("T", p.G, dtok)
        if stage < 0.85:
            return []

        p.phase = f"{pass_id}g{g}_hh"
        hh_v = h_prod(1)

        res = []

        def do_res(sub):
            p.phase = f"{pass_id}g{g}_res"
            skip = (pass_id == "p1" and g == 0 and sub == 0)
            xslot = b_x[:, (sub % 2) * D:(sub % 2 + 1) * D]
            p.wait("Y", p.V, st.get(f"xresfree{sub % 2}", 0))
            p.wait("Y", p.S, st.get(f"xresfreeS{sub % 2}", 0))
            p.waitt("Y", st.get(f"xresfreeG{sub % 2}"))
            p.waitt("Y", st.get(f"xstore{sub % 2}"))
            xl = p.xlA if sub % 2 == 0 else p.xlB
            ytok = p.dma("Y", xslot, x_resid_src(sub), rs=xl)
            p.wait("V", xl, ytok)
            tok = p.op("V", lambda e, o=xslot, a=xslot,
                       b=b_acc[:, sub * D:(sub + 1) * D]:
                       e.tensor_add(o, a, b), inc=True)
            res.append((tok, xslot, skip))
            dst = store_dst(sub)
            if dst is not None and not skip:
                p.wait(store_q, p.V, tok)
                if store_q == "G":
                    so = p.oA if sub % 2 == 0 else p.oB
                else:
                    so = p.stA if sub % 2 == 0 else p.stB
                stok = p.dma(store_q, dst, xslot, rs=so)
                if store_q == "G":
                    st[f"xresfreeG{sub % 2}"] = (so, stok)
                else:
                    st[f"xstore{sub % 2}"] = (so, stok)
            if epi is not None:
                epi(sub, tok, xslot)
            else:
                st[f"xresfree{sub % 2}"] = tok

        if stage < 0.9:
            return []
        p.phase = f"{pass_id}g{g}_rt"
        p.wait("T", p.V, hh_v)
        p.wait("T", p.V, wse_tok)
        w2_T_done = {}
        for e_ in range(NE):
            slot = 1 - (e_ % 2)
            p.waitt("T", w2_tok[e_])
            if e_ + 1 < NE:
                war = wout_T_done if e_ == 0 else w2_T_done[e_ - 1]
                w2_tok[e_ + 1] = load_W(1 - slot, f"w2_{e_ + 1}", war)
            for sub in range(nsub):
                par = sub % 2
                pr = PAIR[par]
                ps_wait(list(pr))
                mms = [None, None]
                for jc in range(NJC):
                    for eh in range(2):
                        mms[eh] = p.op(
                            "T",
                            lambda e, o=PS[pr[eh]][:, :],
                            l=hs(jc, 128, sub * 128),
                            r=Wtile(slot, jc, eh * 512, 512),
                            s0=(jc == 0), s1=(jc == NJC - 1):
                            e.matmul(o, l, r, start=s0, stop=s1),
                            inc=(jc == NJC - 1))
                for eh in range(2):
                    p.wait("V", p.T, mms[eh])
                    tok = p.op("V", lambda e,
                               o=b_acc[:, sub * D + eh * 512:
                                       sub * D + eh * 512 + 512],
                               i=PS[pr[eh]][:, :],
                               s=b_wse[:, sub * 8 + e_: sub * 8 + e_ + 1],
                               a=b_acc[:, sub * D + eh * 512:
                                       sub * D + eh * 512 + 512]:
                               e.scalar_tensor_tensor(o, i, s, a, ALU.mult,
                                                      ALU.add), inc=True)
                    ps_used(pr[eh], "V", tok)
                if e_ == NE - 1:
                    do_res(sub)
                    p.phase = f"{pass_id}g{g}_rt"
            w2_T_done[e_] = p.T.n
            if e_ == 6 and at_e6 is not None:
                at_e6(w2_T_done[6])
                p.phase = f"{pass_id}g{g}_rt"
            if weave is not None:
                weave(e_)
                p.phase = f"{pass_id}g{g}_rt"
        st["hfree"] = p.T.n
        st["Wfree0"] = p.T.n
        st["Wfree1"] = p.T.n

        if stage < 0.95:
            return []
        st["accfree"] = res[-1][0]
        return res

    def load_attnw(n0, n1, war_tok):
        if conv_done.get(n1) is None:
            conv(n0)
            cv = conv(n1)
            p.wait("Y", p.cw, cv)
        p.wait("Y", p.T, war_tok)
        tok = None
        for dc in range(DC):
            tok = p.dma("Y", Wtile(1, dc),
                        s_wb[n0][dc * 128:(dc + 1) * 128, :], rs=p.aw)
        for dc in range(DC):
            tok = p.dma("Y", Wtile(1, 8 + dc),
                        s_wb[n1][dc * 128:(dc + 1) * 128, :], rs=p.aw)
        st["awtok"] = tok

    def p1_tail(g, tr2):
        p.phase = f"p1g{g}_tail"
        nsub = 9 if g == 0 else 8
        w = nsub * 128
        own0 = 128 if g == 0 else 0
        ln2s = max((t[1] or 0) for t in tr2)
        if stage < 1.2:
            return

        wk_tok = st["awtok"]

        p.wait("V", p.S, ln2s)
        p.wait("V", p.T, st["hfree"])
        p.wait("V", p.xs, st.get("hfreeY", 0))
        mix_last = None
        for mi, tv in enumerate((2, 3)):
            for dc in range(DC):
                M0 = mi * 8 + dc
                p.op("V", lambda e, o=b_st[:, 16:17], i=tvcol(tv, dc):
                     e.tensor_scalar(o, i, -1.0, 1.0, ALU.mult, ALU.add))
                p.op("V", lambda e, o=hs(M0, w), i=xlnT(dc, 0, w),
                     s=tvcol(tv, dc): e.tensor_scalar_mul(o, i, s))
                tok = p.op("V", lambda e, o=hs(M0, w - 1, 1),
                           i=xlnT(dc, 0, w - 1), s=b_st[:, 16:17],
                           a=hs(M0, w - 1, 1):
                           e.scalar_tensor_tensor(o, i, s, a, ALU.mult,
                                                  ALU.add), inc=True)
                if g > 0:
                    tok = p.op("V", lambda e, o=hs(M0, 1, 0),
                               i=b_bnd[:, dc:dc + 1], s=b_st[:, 16:17],
                               a=hs(M0, 1, 0):
                               e.scalar_tensor_tensor(o, i, s, a, ALU.mult,
                                                      ALU.add), inc=True)
                mix_last = tok
        if stage < 1.25:
            return
        p.phase = f"p1g{g}_kv"
        p.wait("T", p.V, mix_last)
        p.wait("T", p.aw, wk_tok)
        for ec in range(DC):
            ps_wait([2, 3])
            mmk = [None, None]
            for th in range(2):
                for dc in range(DC):
                    mmk[th] = p.op(
                        "T",
                        lambda e, o=PS[2 + th][:, :],
                        l=Wtile(1, dc, ec * 128, 128),
                        r=hs(dc, 512, own0 + th * 512),
                        s0=(dc == 0), s1=(dc == DC - 1):
                        e.matmul(o, l, r, start=s0, stop=s1),
                        inc=(dc == DC - 1))
            ps_wait([5, 6])
            mmv = [None, None]
            for th in range(2):
                for dc in range(DC):
                    mmv[th] = p.op(
                        "T",
                        lambda e, o=PS[5 + th][:, :],
                        l=Wtile(1, 8 + dc, ec * 128, 128),
                        r=hs(8 + dc, 512, own0 + th * 512),
                        s0=(dc == 0), s1=(dc == DC - 1):
                        e.matmul(o, l, r, start=s0, stop=s1),
                        inc=(dc == DC - 1))
            ekslot, vslot = ec % 2, 2 + ec % 2
            dump = b_acc[:, 8 * D:9 * D]
            p.waitt("S", st.get(f"spillek{ekslot}"))
            ektok = None
            for th in range(2):
                p.wait("S", p.T, mmk[th])
                ektok = p.op("S", lambda e, o=F(ekslot, 512, th * 512),
                             i=PS[2 + th][:, :]:
                             e.activation(o, i, AF.Exp), inc=True)
                ps_used(2 + th, "S", ektok)
            p.waitt("V", st.get(f"spillekv{vslot}"))
            vtok = None
            for th in range(2):
                p.wait("V", p.T, mmv[th])
                vtok = p.op("V", lambda e, o=F(vslot, 512, th * 512),
                            i=PS[5 + th][:, :]: e.tensor_copy(o, i), inc=True)
                ps_used(5 + th, "V", vtok)
            p.wait("V", p.S, ektok)
            ekvtok = p.op("V", lambda e, o=F(vslot), a=F(ekslot), b=F(vslot):
                          e.tensor_mul(o, a, b), inc=True)
            if stage < 1.35:
                continue
            p.op("V", lambda e, o=dump,
                 d0=drv(1, ec).broadcast_to((128, GT)), d1=F(vslot),
                 i0=b_ends[:, ec:ec + 1]:
                 e.tensor_tensor_scan(o, d0, d1, i0, ALU.mult, ALU.add))
            p.op("V", lambda e, o=b_ends[:, ec:ec + 1],
                 i=dump[:, GT - 1:GT]: e.tensor_copy(o, i))
            sbk = p.op("V", lambda e, o=dump,
                       d0=drv(1, ec).broadcast_to((128, GT)), d1=F(ekslot),
                       i0=b_ends[:, 8 + ec:8 + ec + 1]:
                       e.tensor_tensor_scan(o, d0, d1, i0, ALU.mult, ALU.add),
                       inc=True)
            endtok = p.op("V", lambda e, o=b_ends[:, 8 + ec:8 + ec + 1],
                          i=dump[:, GT - 1:GT]: e.tensor_copy(o, i),
                          inc=True)
            st["endtok"] = endtok
            if stage < 1.45:
                continue
            p.wait("G", p.V, ekvtok)
            spv = getattr(p, f"sp{vslot}")
            st[f"spillekv{vslot}"] = (spv, p.dma(
                "G", s_ekv[ec][:, g * GT:(g + 1) * GT], F(vslot), rs=spv))
            p.wait("G", p.V, sbk)
            spk = getattr(p, f"sp{ekslot}")
            st[f"spillek{ekslot}"] = (spk, p.dma(
                "G", s_ek[ec][:, g * GT:(g + 1) * GT], F(ekslot), rs=spk))
        proj_T = p.T.n
        if stage < 1.49:
            return
        p.wait("V", p.T, proj_T)
        mixr_last = None
        for dc in range(DC):
            p.op("V", lambda e, o=b_st[:, 16:17], i=tvcol(4, dc):
                 e.tensor_scalar(o, i, -1.0, 1.0, ALU.mult, ALU.add))
            p.op("V", lambda e, o=hs(dc, w), i=xlnT(dc, 0, w),
                 s=tvcol(4, dc): e.tensor_scalar_mul(o, i, s))
            tok = p.op("V", lambda e, o=hs(dc, w - 1, 1),
                       i=xlnT(dc, 0, w - 1), s=b_st[:, 16:17],
                       a=hs(dc, w - 1, 1):
                       e.scalar_tensor_tensor(o, i, s, a, ALU.mult,
                                              ALU.add), inc=True)
            if g > 0:
                tok = p.op("V", lambda e, o=hs(dc, 1, 0),
                           i=b_bnd[:, dc:dc + 1], s=b_st[:, 16:17],
                           a=hs(dc, 1, 0):
                           e.scalar_tensor_tensor(o, i, s, a, ALU.mult,
                                                  ALU.add), inc=True)
            mixr_last = tok
        if stage < 1.493:
            return
        for dc in range(DC):
            p.op("V", lambda e, o=b_bnd[:, dc:dc + 1],
                 i=xlnT(dc, w - 1, 1): e.tensor_copy(o, i))
        if stage < 1.497:
            return
        hfY = None
        for dc in range(DC):
            p.wait("Y", p.V, mixr_last if dc == 0 else 0)
            hfY = p.dma("Y", s_xr[dc][:, g * GT:(g + 1) * GT],
                        hs(dc, GT, own0), rs=p.xs)
        st["hfreeY"] = hfY
        st["hfree"] = p.T.n
        st["Wfree1"] = p.T.n

    def p2c_A_chunk(g, dc):
        p.phase = f"p2cg{g}_scan"
        if g == 0 and dc == 0:
            for sem in (p.xs, p.sp0, p.sp1, p.sp2, p.sp3):
                p.wait("Y", sem, sem.n)
            p.wait("Y", p.cw, p.cw.n)
        p.wait("Y", p.V, st.get("bflfreeA", 0))
        p.dma("Y", b_bfl[:, 0:D], s_ekv[dc][:, g * GT:(g + 1) * GT], rs=p.bflA)
        ek_l = p.dma("Y", b_bfl[:, D:2 * D],
                     s_ek[dc][:, g * GT:(g + 1) * GT], rs=p.bflA)
        ekvap = b_bfl[:, 0:D]
        ekap = b_bfl[:, D:2 * D]
        p.wait("V", p.bflA, ek_l)
        p.op("V", lambda e, o=F(0, 1, 0), i=b_cin[:, dc:dc + 1]:
             e.tensor_copy(o, i))
        p.op("V", lambda e, o=F(0, GT, 1),
             d0=drv(1, dc).broadcast_to((128, GT)), d1=ekvap,
             i0=b_cin[:, dc:dc + 1]:
             e.tensor_tensor_scan(o, d0, d1, i0, ALU.mult, ALU.add))
        p.op("V", lambda e, o=b_cin[:, dc:dc + 1], i=F(0, 1, GT):
             e.tensor_copy(o, i))
        p.waitt("V", st.get("wkvsp_prev"))
        p.op("V", lambda e, o=F(1, 1, 0), i=b_cin[:, 8 + dc:8 + dc + 1]:
             e.tensor_copy(o, i))
        p.op("V", lambda e, o=F(1, GT, 1),
             d0=drv(1, dc).broadcast_to((128, GT)), d1=ekap,
             i0=b_cin[:, 8 + dc:8 + dc + 1]:
             e.tensor_tensor_scan(o, d0, d1, i0, ALU.mult, ALU.add))
        p.op("V", lambda e, o=b_cin[:, 8 + dc:8 + dc + 1], i=F(1, 1, GT):
             e.tensor_copy(o, i))
        p.op("V", lambda e, o=F(2), i=ekvap, s=drv(2, dc), a=F(0, GT, 0):
             e.scalar_tensor_tensor(o, i, s, a, ALU.mult, ALU.add))
        dtok = p.op("V", lambda e, o=F(3), i=ekap, s=drv(2, dc),
                    a=F(1, GT, 0):
                    e.scalar_tensor_tensor(o, i, s, a, ALU.mult, ALU.add),
                    inc=True)
        st["bflfreeA"] = dtok
        vtok = p.op("V", lambda e, o=F(4), i=F(3):
                    e.tensor_single_scalar(o, i, 1e-6, ALU.max), inc=True)
        p.wait("S", p.V, vtok)
        sln = p.op("S", lambda e, o=F(3), i=F(4):
                   e.activation(o, i, AF.Ln), inc=True)
        p.wait("S", p.S, sln)
        srd = p.op("S", lambda e, o=F(4), i=F(3):
                   e.activation(o, i, AF.Exp, scale=-1.0), inc=True)
        p.wait("V", p.S, srd)
        wtok = p.op("V", lambda e, o=F(1), a=F(2), b=F(4):
                    e.tensor_mul(o, a, b), inc=True)
        p.wait("G", p.V, wtok)
        sp = p.dma("G", s_wkv[dc][:, g * GT:(g + 1) * GT], F(1), rs=p.spw)
        st["wkvsp_prev"] = (p.spw, sp)

    def p2c_B(g):
        p.phase = f"p2cg{g}"
        nsub = 8
        if g == 0:
            load_attnw("wr", "wo", st["Wfree1"])
        wr_tok = st["awtok"]
        p.wait("Y", p.V, st.get("accfree", 0))
        xr_tok = None
        for dc in range(DC):
            xr_tok = p.dma("Y", b_accb[:, dc * D:(dc + 1) * D],
                           s_xr[dc][:, g * GT:(g + 1) * GT], rs=p.xrl)

        p.wait("T", p.aw, wr_tok)
        p.wait("T", p.xrl, xr_tok)
        rs_tok = None
        for ec in range(DC):
            ps_wait([2, 3])
            mm = [None, None]
            for th in range(2):
                for dc in range(DC):
                    mm[th] = p.op(
                        "T",
                        lambda e, o=PS[2 + th][:, :],
                        l=Wtile(1, dc, ec * 128, 128),
                        r=b_accb[:, dc * D + th * 512: dc * D + th * 512 + 512],
                        s0=(dc == 0), s1=(dc == DC - 1):
                        e.matmul(o, l, r, start=s0, stop=s1),
                        inc=(dc == DC - 1))
            for th in range(2):
                p.wait("S", p.T, mm[th])
                rs_tok = p.op("S", lambda e, o=hs(8 + ec, 512, th * 512),
                              i=PS[2 + th][:, :]:
                              e.activation(o, i, AF.Sigmoid), inc=True)
                ps_used(2 + th, "S", rs_tok)
        st["xr_T_done"] = p.T.n

        p.wait("Y", p.spw, p.spw.n)
        rwkv_tok = None
        for dc in range(DC):
            par = dc % 2
            p.wait("Y", p.V, st.get(f"bflBfree{par}", 0))
            wl = p.dma("Y", b_bfl[:, (2 + par) * D:(3 + par) * D],
                       s_wkv[dc][:, g * GT:(g + 1) * GT], rs=p.bflB)
            p.wait("V", p.bflB, wl)
            p.wait("V", p.S, rs_tok if dc == 0 else 0)
            rwkv_tok = p.op("V", lambda e, o=hs(dc, GT),
                            a=b_bfl[:, (2 + par) * D:(3 + par) * D],
                            b=hs(8 + dc, GT): e.tensor_mul(o, a, b), inc=True)
            st[f"bflBfree{par}"] = rwkv_tok
        st["fin"] = rwkv_tok

        p.phase = f"p2cg{g}_wo"
        p.wait("T", p.V, rwkv_tok)
        tr3 = []
        for sub in range(nsub):
            par = sub % 2
            pr = PAIR[par]
            ps_wait(list(pr))
            mm = [None, None]
            for dc in range(DC):
                for eh in range(2):
                    mm[eh] = p.op(
                        "T",
                        lambda e, o=PS[pr[eh]][:, :],
                        l=hs(dc, 128, sub * 128),
                        r=Wtile(1, 8 + dc, eh * 512, 512),
                        s0=(dc == 0), s1=(dc == DC - 1):
                        e.matmul(o, l, r, start=s0, stop=s1),
                        inc=(dc == DC - 1))
            slot = b_x[:, (sub % 2) * D:(sub % 2 + 1) * D]
            row = g * GT + sub * 128
            p.wait("Y", p.V, st.get(f"xresfree{sub % 2}", 0))
            p.wait("Y", p.S, st.get(f"xresfreeS{sub % 2}", 0))
            p.waitt("Y", st.get(f"xresfreeG{sub % 2}"))
            p.waitt("Y", st.get(f"xstore{sub % 2}"))
            xl = p.xlA if sub % 2 == 0 else p.xlB
            ytok = p.dma("Y", slot, s_x1[row:row + 128, :], rs=xl)
            p.wait("V", xl, ytok)
            tok = None
            for eh in range(2):
                p.wait("V", p.T, mm[eh])
                tok = p.op("V", lambda e, o=slot[:, eh * 512:(eh + 1) * 512],
                           a=slot[:, eh * 512:(eh + 1) * 512],
                           b=PS[pr[eh]][:, :]: e.tensor_add(o, a, b), inc=True)
                ps_used(pr[eh], "V", tok)
            p.wait("Y", p.V, tok)
            sx = p.stA if sub % 2 == 0 else p.stB
            stok = p.dma("Y", s_x2[row:row + 128, :], slot, rs=sx)
            st[f"xstore{sub % 2}"] = (sx, stok)
            tr3.append(ln_transpose(sub, slot))
            st[f"xresfree{sub % 2}"] = p.V.n
            st[f"xresfreeS{sub % 2}"] = p.S.n
        st["Wfree0"] = p.T.n
        st["Wfree1"] = p.T.n
        st["hfree"] = p.T.n
        return tr3

    def xin_row(g, sub):
        return g * GT + sub * 128 + (0 if g == 0 else 128)

    def own_row(g, sub):
        return g * GT + sub * 128 - (128 if g == 0 else 0)

    for g in range([0, NG if stage >= 2 else 1][stage >= 0.12]):
        p.phase = f"p1g{g}_head"
        nsub = 9 if g == 0 else 8
        if stage < 0.3:
            nsub = 1
        tr = []
        for sub in range(nsub):
            slot = b_x[:, (sub % 2) * D:(sub % 2 + 1) * D]
            p.wait("Y", p.V, st.get(f"xresfree{sub % 2}", 0))
            p.wait("Y", p.S, st.get(f"xresfreeS{sub % 2}", 0))
            p.waitt("Y", st.get(f"xstore{sub % 2}"))
            xl = p.xlA if sub % 2 == 0 else p.xlB
            ytok = p.dma("Y", slot,
                         d_xin[xin_row(g, sub): xin_row(g, sub) + 128, :],
                         rs=xl)
            p.wait("V", xl, ytok)
            p.wait("S", xl, ytok)
            if sub == 0 and g == 0:
                p.wait("T", p.V, st["bnd0"])
            tr.append(ln_transpose(sub, slot))
            st[f"xresfree{sub % 2}"] = p.V.n
            st[f"xresfreeS{sub % 2}"] = p.S.n

        tr2 = []

        def epi_p1(sub, tok, slot):
            if stage < 1.1:
                st[f"xresfree{sub % 2}"] = tok
                return
            p.wait("S", p.V, tok)
            if stage >= 1.145:
                parts = "full"
            elif stage >= 1.1405:
                nfull = {1.141: 1, 1.142: 2, 1.143: 4, 1.144: 6}.get(stage, 0)
                parts = "full" if sub < nfull else "tr"
            elif stage >= 1.14:
                parts = "tr"
            else:
                parts = "ln"
            tr2.append(ln_transpose(sub, slot, parts=parts))
            st[f"xresfree{sub % 2}"] = p.V.n
            st[f"xresfreeS{sub % 2}"] = p.S.n

        res = ffn_core(
            "p1", g,
            lambda sub, g=g: d_xin[xin_row(g, sub): xin_row(g, sub) + 128, :],
            lambda sub, g=g: None if (g == 0 and sub == 0)
            else s_x1[own_row(g, sub): own_row(g, sub) + 128, :],
            tr, epi=epi_p1,
            at_e6=lambda war: load_attnw("wk", "wv", war))
        if stage >= 1.1 and res:
            p1_tail(g, tr2)

    p.phase = "xchg"
    if stage >= 3:
        stg = p.op("V", lambda e, o=b_stage[:, :], i=b_ends[:, :],
                   s=b_mask[:, 0:1]: e.tensor_scalar_mul(o, i, s), inc=True)
        p.wait("G", p.V, stg)
        gtok = p.dma("G", cc_in[:, :], b_stage[:, :], rs=p.cc2)
        p.wait("G", p.cc2, gtok)
        p.CC.n += 1
        p.st["G"].append((p.phase,
            lambda e: e.collective_compute(
                "AllReduce", ALU.add,
                replica_groups=[[0, 1], [2, 3], [4, 5], [6, 7]],
                ins=[cc_in.ap().opt()], outs=[cc_out.ap().opt()],
            ).then_inc(p.CC.h, 1)))
        p.wait("G", p.CC, p.CC.n)
        cin_tok = p.dma("G", b_cin[:, :], cc_out[:, :], rs=p.cc2)
        p.wait("V", p.cc2, cin_tok)
        cin_v = p.op("V", lambda e, o=b_cin[:, :], i=b_cin[:, :],
                     s=b_mask[:, 1:2]: e.tensor_scalar_mul(o, i, s), inc=True)
        st["cin"] = cin_v

    conv("wr")
    conv("wo")
    if stage >= 4:
        for g in range(NG if stage >= 5 else 1):
            if g == 0:
                for dc_ in range(DC):
                    p2c_A_chunk(0, dc_)
            tr3 = p2c_B(g)
            ng2 = NG if stage >= 5 else 1
            weave = ((lambda i, gg=g + 1: p2c_A_chunk(gg, i))
                     if g + 1 < ng2 else None)
            res = ffn_core(
                "p4", g,
                lambda sub, g=g: s_x2[g * GT + sub * 128:
                                      g * GT + sub * 128 + 128, :],
                lambda sub, g=g: d_y[g * GT + sub * 128:
                                     g * GT + sub * 128 + 128, :],
                tr3, store_q="G", weave=weave,
                at_e6=((lambda war: load_attnw("wr", "wo", war))
                       if g + 1 < (NG if stage >= 5 else 1) else None))
            st["out_last"] = 1

    if debug:
        for r0 in range(0, TOWN, 1024):
            p.dma("G", dbg_x1[r0:r0 + 1024, :], s_x1[r0:r0 + 1024, :])
            p.dma("G", dbg_x2[r0:r0 + 1024, :], s_x2[r0:r0 + 1024, :])
        p.dma("G", dbg_sm[:, 0:16], b_ends[:, :])
        p.dma("G", dbg_sm[:, 16:32], b_cin[:, :])
        p.dma("G", dbg_sm[:, 32:36], b_bnd[:, 0:8].bitcast(F32))
        tok = p.dma("G", dbg_ekv[:, 0:2048], s_ekv[0].bitcast(F32))
        st["out_last"] = max(st.get("out_last", 0), tok)

    for s_, sems in (
        ("Y", (p.Y, p.ct, p.cw, p.winS0, p.winS1, p.aw, p.xrl, p.bflA,
               p.bflB, p.xlA, p.xlB, p.xs, p.stA, p.stB)),
        ("G", (p.G, p.wg, p.cr, p.sp0, p.sp1, p.sp2, p.sp3, p.cc2,
               p.oA, p.oB)),
        ("S", (p.SQ, p.wq0, p.wq1)),
    ):
        for sem in sems:
            p.wait(s_, sem, sem.n)

    _blk = nc.Block()
    block = _blk.__enter__()
    for name, attr in (("Y", "sync"), ("G", "gpsimd"), ("T", "tensor"),
                       ("V", "vector"), ("S", "scalar")):
        def run_ops(eng, ops=p.st[name]):
            cur = [None, None]

            def close():
                if cur[1] is not None:
                    cur[1].__exit__(None, None, None)
                    cur[1] = None
            for ph, f in ops:
                if ph != cur[0]:
                    close()
                    cur[0] = ph
                    cur[1] = nc.named_scope(ph)
                    cur[1].__enter__()
                f(eng)
            close()
        getattr(block, attr)(run_ops)
    _blk.__exit__(None, None, None)
    return nc


def _stage_inputs(inputs):
    x = np.asarray(inputs["L0_attn_x"], np.float32)
    win_t = np.ascontiguousarray(np.asarray(inputs["L0_ffn_w_in"],
                                            np.float32).T)
    w1_t = np.ascontiguousarray(np.asarray(inputs["L0_ffn_w1"], np.float32).T)
    wout_t = np.ascontiguousarray(np.asarray(inputs["L0_ffn_w_out"],
                                             np.float32).T)
    w2_t = np.ascontiguousarray(
        np.asarray(inputs["L0_ffn_w2"], np.float32).transpose(0, 2, 1))
    wge_t = np.ascontiguousarray(np.concatenate(
        [np.asarray(inputs["L0_ffn_wg"], np.float32).T,
         np.asarray(inputs["L0_ffn_we"], np.float32).T], axis=1))
    gbeb = np.ascontiguousarray(np.broadcast_to(np.concatenate(
        [np.asarray(inputs["L0_ffn_gb"], np.float32),
         np.asarray(inputs["L0_ffn_eb"], np.float32)]), (128, 10)))
    wk_t = np.ascontiguousarray(np.asarray(inputs["L0_attn_Wk"], np.float32).T)
    wv_t = np.ascontiguousarray(np.asarray(inputs["L0_attn_Wv"], np.float32).T)
    wr_t = np.ascontiguousarray(np.asarray(inputs["L0_attn_Wr"], np.float32).T)
    wo_t = np.ascontiguousarray(np.asarray(inputs["L0_attn_Wo"], np.float32).T)
    tvec = np.ascontiguousarray(np.stack([
        np.asarray(inputs["L0_attn_time_decay"], np.float32),
        np.asarray(inputs["L0_attn_time_first"], np.float32),
        np.asarray(inputs["L0_attn_time_mix_k"], np.float32),
        np.asarray(inputs["L0_attn_time_mix_v"], np.float32),
        np.asarray(inputs["L0_attn_time_mix_r"], np.float32),
    ]).reshape(5, DC, 128, 1))
    ident = np.ascontiguousarray(np.eye(128, dtype=bfloat16))

    in_maps = []
    for c in range(8):
        b, h = c // 2, c % 2
        if h == 0:
            prev = np.zeros((128, D), np.float32)
            own = x[b, 0:TOWN]
        else:
            prev = x[b, TOWN - 128:TOWN]
            own = x[b, TOWN:2 * TOWN]
        xin = np.ascontiguousarray(np.concatenate([prev, own], axis=0))
        masks = np.zeros((128, 2), np.float32)
        masks[:, 0] = 1.0 if h == 0 else 0.0
        masks[:, 1] = 1.0 if h == 1 else 0.0
        in_maps.append({
            "xin": xin, "win_t": win_t, "w1_t": w1_t, "wout_t": wout_t,
            "w2_t": w2_t, "wge_t": wge_t, "gbeb": gbeb,
            "wk_t": wk_t, "wv_t": wv_t, "wr_t": wr_t, "wo_t": wo_t,
            "tvec": tvec, "masks": masks, "ident": np.asarray(ident),
        })
    return in_maps


def kernel(**inputs):
    if "nc" not in _CACHE:
        _CACHE["nc"] = build_program()
    nc = _CACHE["nc"]
    in_maps = _stage_inputs(inputs)
    res = run_bass_kernel_spmd(nc, in_maps, core_ids=list(range(8)))
    out = np.empty((4, 2 * TOWN, D), np.float32)
    for c in range(8):
        b, h = c // 2, c % 2
        out[b, h * TOWN:(h + 1) * TOWN] = res.results[c]["y"]
    return out
